# version 26
# baseline (speedup 1.0000x reference)
import numpy as np
import ml_dtypes
import concourse.bass as bass
import concourse.mybir as mybir
from concourse.bass_utils import run_bass_kernel_spmd

F32 = mybir.dt.float32
BF16 = mybir.dt.bfloat16
AF = mybir.ActivationFunctionType
ALU = mybir.AluOpType
AX = mybir.AxisListType

D = 1024
I = 2816
TOKA = 8192
TOKC = 4096
NCA = TOKA // 128
NCC = TOKC // 128
DC = 8
IC = 22
EPS = 1e-6
LNEPS = 1e-5

ENGS = ("sy", "te", "ve", "sc", "po")
SEMS = ENGS + ("pd",)


class Op:
    __slots__ = ("ek", "sem", "fn", "inc", "need_inc", "semval", "waits")

    def __init__(self, ek, sem, fn, inc):
        self.ek, self.sem, self.fn, self.inc = ek, sem, fn, inc
        self.need_inc = inc == 16
        self.semval = None
        self.waits = []


class Prog:

    def __init__(self, nc):
        self.nc = nc
        self.sems = {e: nc.ctx.enter_context(nc.semaphore(f"s_{e}"))
                     for e in SEMS}
        self.ops = []
        self.counts = {e: 0 for e in SEMS}
        self.last_write = {}
        self.readers = {}
        self.fences = {}

    def op(self, ek, fn, reads=(), writes=(), inc=1, sem=None):
        ps_reads = tuple(k for k in reads if k.startswith("ps"))
        if ps_reads:
            reads = tuple(k for k in reads if not k.startswith("ps"))
            writes = tuple(writes) + ps_reads
        o = Op(ek, sem or ek, fn, inc)
        deps = []
        for k in reads:
            w = self.last_write.get(k)
            if w is not None:
                deps.append(w)
        for k in writes:
            w = self.last_write.get(k)
            if w is not None:
                deps.append(w)
            deps.extend(self.readers.get(k, {}).values())
        for d in deps:
            if d is o:
                continue
            if d.ek == "te" and ek == "te":
                continue
            if d not in o.waits:
                o.waits.append(d)
                d.need_inc = True
        for k in reads:
            self.readers.setdefault(k, {})[ek] = o
        for k in writes:
            self.last_write[k] = o
            self.readers[k] = {}
        self.ops.append(o)
        return o

    def dma(self, out, in_, reads=(), writes=(), queue="sy"):
        if queue == "sy":
            chain, sem = "_dmA", "sy"
        else:
            chain, sem = "_dmB", "pd"
        return self.op(queue, lambda e: e.dma_start(out=out, in_=in_),
                       reads=tuple(reads) + (chain,),
                       writes=tuple(writes) + (chain,), inc=16, sem=sem)

    def mm(self, out, lhsT, rhs, start, stop, reads=(), writes=()):
        return self.op("te", lambda e: e.matmul(out, lhsT, rhs, start=start,
                                                stop=stop),
                       reads=reads, writes=writes)

    def tr(self, out, in_, ident, reads=(), writes=()):
        return self.op("te", lambda e: e.transpose(out, in_, ident),
                       reads=reads, writes=writes)

    def play(self):
        ops, self.ops = self.ops, []
        if not ops:
            return
        lastop = {}
        for o in ops:
            lastop[o.sem] = o
        for o in lastop.values():
            o.need_inc = True
        for o in ops:
            if o.need_inc:
                self.counts[o.sem] += o.inc
                o.semval = self.counts[o.sem]
        with self.nc.Block() as block:
            sections = {"sy": block.sync, "te": block.tensor,
                        "ve": block.vector, "sc": block.scalar,
                        "po": block.gpsimd}
            for ek, section in sections.items():
                mine = [o for o in ops if o.ek == ek]
                if not mine:
                    continue
                sems = self.sems

                def body(eng, mine=mine, sems=sems):
                    water = {e: 0 for e in SEMS}
                    for o in mine:
                        for d in o.waits:
                            v = d.semval
                            assert v is not None
                            if v > water[d.sem]:
                                eng.wait_ge(sems[d.sem], v)
                                water[d.sem] = v
                        ins = o.fn(eng)
                        if o.need_inc:
                            ins.then_inc(sems[o.sem], o.inc)

                section(body)
        fence = dict(lastop)
        for k in list(self.last_write):
            w = self.last_write[k]
            self.last_write[k] = fence.get(w.sem, w)
        for k in list(self.readers):
            self.readers[k] = {e: fence.get(o.sem, o)
                               for e, o in self.readers[k].items()}

    def finish(self):
        counts = dict(self.counts)
        with self.nc.Block() as block:
            def body(eng):
                for e in SEMS:
                    if counts[e]:
                        eng.wait_ge(self.sems[e], counts[e])
            block.sync(body)


def t_bands(pfx, c0, w):
    bands = []
    for b, lo, hi in ((pfx + "a", 0, 1024), (pfx + "b", 1024, 2048),
                      (pfx + "c", 2048, 2816)):
        if c0 < hi and c0 + w > lo:
            bands.append(b)
    return tuple(bands)


def xb_bands(c0, w):
    bands = []
    for b, lo, hi in (("Xb0", 0, 1024), ("Xb1", 1024, 2048),
                      ("Xb2b", 2048, 2816)):
        if c0 < hi and c0 + w > lo:
            bands.append(b)
    return tuple(bands)


def build(cc_groups=((0, 1), (2, 3), (4, 5), (6, 7))):
    cc_groups = [list(g) for g in cc_groups]
    nc = bass.Bass("TRN2", target_bir_lowering=False)

    xb_d = nc.dram_tensor("xb", [TOKA, D], F32, kind="ExternalInput")
    wqt_d = nc.dram_tensor("wqt", [D, D], F32, kind="ExternalInput")
    wkt_d = nc.dram_tensor("wkt", [D, D], F32, kind="ExternalInput")
    wv_d = nc.dram_tensor("wv", [D, D], F32, kind="ExternalInput")
    bq_d = nc.dram_tensor("bq", [D], F32, kind="ExternalInput")
    bk_d = nc.dram_tensor("bk", [D], F32, kind="ExternalInput")
    bv_d = nc.dram_tensor("bv", [D], F32, kind="ExternalInput")
    gam_d = nc.dram_tensor("gam", [1], F32, kind="ExternalInput")
    wg_d = nc.dram_tensor("wg", [I, D], F32, kind="ExternalInput")
    wu_d = nc.dram_tensor("wu", [I, D], F32, kind="ExternalInput")
    wd_d = nc.dram_tensor("wd", [D, I], F32, kind="ExternalInput")
    ident_d = nc.dram_tensor("ident", [128, 128], BF16, kind="ExternalInput")
    out_d = nc.dram_tensor("out", [TOKC, D], F32, kind="ExternalOutput")
    wgq_sp = nc.dram_tensor("wgq_sp", [I, D], BF16, kind="Internal")
    wuq_sp = nc.dram_tensor("wuq_sp", [I, D], BF16, kind="Internal")
    wdq_sp = nc.dram_tensor("wdq_sp", [D, I], BF16, kind="Internal")
    cc_in_d = nc.dram_tensor("cc_in", [128, DC * D + 24], F32,
                             kind="Internal")
    cc_out_d = nc.dram_tensor("cc_out", [128, DC * D + 24], F32,
                              kind="Internal")

    from contextlib import ExitStack
    with ExitStack() as ctx:
        A = ctx.enter_context

        x_t = [A(nc.sbuf_tensor(f"x_t{i}", [128, D], F32)) for i in (0, 1)]
        xn_f = A(nc.sbuf_tensor("xn_f", [128, D], F32))
        xn_bf = A(nc.sbuf_tensor("xn_bf", [128, D], BF16))
        xnfm = A(nc.sbuf_tensor("xnfm", [128, D], BF16))
        xqfm = A(nc.sbuf_tensor("xqfm", [128, D], BF16))
        Xb = A(nc.sbuf_tensor("Xb", [128, I], F32))
        u_b = A(nc.sbuf_tensor("u_b", [128, D], F32))
        t1 = A(nc.sbuf_tensor("t1", [128, I], BF16))
        t2 = A(nc.sbuf_tensor("t2", [128, I], BF16))
        sgn = A(nc.sbuf_tensor("sgn", [128, D], BF16))
        g_bf = A(nc.sbuf_tensor("g_bf", [128, D], BF16))
        r16 = A(nc.sbuf_tensor("r16", [128, I // 16], F32))
        scl = A(nc.sbuf_tensor("scl", [128, I // 16], F32))
        mcol = A(nc.sbuf_tensor("mcol", [128, NCC], F32))
        rcol = A(nc.sbuf_tensor("rcol", [128, NCC], F32))
        sm1 = A(nc.sbuf_tensor("sm1", [128, 1], F32))
        sm2 = A(nc.sbuf_tensor("sm2", [128, 1], F32))
        sm3 = A(nc.sbuf_tensor("sm3", [128, 1], F32))
        sm4 = A(nc.sbuf_tensor("sm4", [128, 1], F32))
        gam_col = A(nc.sbuf_tensor("gam_col", [128, 1], F32))
        bvs_col = A(nc.sbuf_tensor("bvs_col", [128, 1], F32))
        idt = A(nc.sbuf_tensor("idt", [128, 128], BF16))
        ones_col = A(nc.sbuf_tensor("ones_col", [128, 1], BF16))
        ones_row = A(nc.sbuf_tensor("ones_row", [1, 128], BF16))
        bv_col = A(nc.sbuf_tensor("bv_col", [128, 8], BF16))
        small = A(nc.sbuf_tensor("small", [128, 24], F32))
        nq_c = A(nc.sbuf_tensor("nq_c", [128, 8], F32))
        inq_c = A(nc.sbuf_tensor("inq_c", [128, 8], F32))
        ink_c = A(nc.sbuf_tensor("ink_c", [128, 8], F32))
        wvs_c = A(nc.sbuf_tensor("wvs_c", [128, 8], BF16))
        nq_row = A(nc.sbuf_tensor("nq_row", [1, D], BF16))
        rs_row = A(nc.sbuf_tensor("rs_row", [1, D], BF16))
        sg_row = A(nc.sbuf_tensor("sg_row", [1, D], BF16))
        sig_bc = A(nc.sbuf_tensor("sig_bc", [128, D], BF16))
        g1t = A(nc.sbuf_tensor("g1t", [128, DC * D], BF16))
        vs_row = A(nc.sbuf_tensor("vs_row", [1, 128], BF16))

        p = Prog(nc)

        def emit_quant_half(X, Xkey, F, c0, filler=False):
            if isinstance(Xkey, str):
                Xkey = (Xkey,)
            G = F // 16
            g0 = c0 // 16
            Xs = X[:, c0:c0 + F]
            Xg = Xs.rearrange("p (g k) -> p g k", k=16)
            t1s = t1[:, c0:c0 + F]
            t2s = t2[:, c0:c0 + F]
            t1k = t_bands("t1", c0, F)
            t2k = t_bands("t2", c0, F)
            ukeys = ("u0",) if F <= 512 else ("u0", "u1")
            e_mul = "ve"
            p.op("ve", lambda e: e.reduce_max(r16[:, g0:g0 + G], Xg,
                                              axis=AX.X,
                                              apply_absolute_value=True),
                 reads=Xkey, writes=("r16",))
            p.op("ve", lambda e: e.tensor_scalar(scl[:, g0:g0 + G],
                                                 r16[:, g0:g0 + G],
                                                 1.0 / 12.0, 5e-13,
                                                 ALU.mult, ALU.max),
                 reads=("r16",), writes=("scl",))
            p.op("ve", lambda e: e.reciprocal(r16[:, g0:g0 + G],
                                              scl[:, g0:g0 + G]),
                 reads=("scl",), writes=("r16",))
            p.op("sc", lambda e: e.activation(u_b[:, 0:F], Xs, AF.Abs,
                                              scale=0.5),
                 reads=Xkey, writes=ukeys)
            p.op("sc", lambda e: e.activation(sgn[:, 0:F], Xs, AF.Sign),
                 reads=Xkey, writes=("sgn",))
            rbc = r16[:, g0:g0 + G].unsqueeze(2).broadcast_to([128, G, 16])
            ug = u_b[:, 0:F].rearrange("p (g k) -> p g k", k=16)
            p.op(e_mul, lambda e: e.tensor_tensor(ug, ug, rbc, ALU.mult),
                 reads=ukeys + ("r16",), writes=ukeys)
            uf = u_b[:, 0:F]
            p.op(e_mul, lambda e: e.tensor_scalar(t1s, uf, 5.0, 2.0,
                                                  ALU.is_gt, ALU.mult),
                 reads=ukeys, writes=t1k)
            chain = [(3.5, t1s, t2s, t1k, t2k), (2.5, t2s, t1s, t2k, t1k),
                     (None, t1s, t2s, t1k, t2k),
                     (1.75, t2s, t1s, t2k, t1k),
                     (1.25, t1s, t2s, t1k, t2k),
                     (0.75, t2s, t1s, t2k, t1k),
                     (0.25, t1s, t2s, t1k, t2k)]
            for thr, sb, db, skk, dkk in chain:
                if thr is None:
                    p.op("ve", lambda e, sb=sb, db=db: e.tensor_scalar(
                        db, sb, 2.0, None, ALU.mult),
                        reads=skk, writes=dkk)
                else:
                    p.op("ve", lambda e, thr=thr, sb=sb, db=db:
                         e.scalar_tensor_tensor(db, uf, thr, sb, ALU.is_gt,
                                                ALU.add),
                         reads=ukeys + skk, writes=dkk)
            t2g = t2s.rearrange("p (g k) -> p g k", k=16)
            e_fin = "ve"
            p.op(e_fin, lambda e: e.tensor_tensor(t2s, t2s, sgn[:, 0:F],
                                                  ALU.mult),
                 reads=t2k + ("sgn",), writes=t2k)
            sbc = scl[:, g0:g0 + G].unsqueeze(2).broadcast_to([128, G, 16])
            p.op(e_fin, lambda e: e.tensor_tensor(t2g, t2g, sbc, ALU.mult),
                 reads=t2k + ("scl",), writes=t2k)

        def emit_wq_pre(F, c0, sg, sgk):
            G = F // 16
            g0 = c0 // 16
            Xkey = xb_bands(c0, F)
            Xs = Xb[:, c0:c0 + F]
            Xg = Xs.rearrange("p (g k) -> p g k", k=16)
            p.op("ve", lambda e: e.reduce_max(r16[:, g0:g0 + G], Xg,
                                              axis=AX.X,
                                              apply_absolute_value=True),
                 reads=Xkey, writes=("r16",))
            p.op("ve", lambda e: e.tensor_scalar(scl[:, g0:g0 + G],
                                                 r16[:, g0:g0 + G],
                                                 1.0 / 12.0, 5e-13,
                                                 ALU.mult, ALU.max),
                 reads=("r16",), writes=("scl",))
            p.op("ve", lambda e: e.reciprocal(r16[:, g0:g0 + G],
                                              scl[:, g0:g0 + G]),
                 reads=("scl",), writes=("r16",))
            p.op("sc", lambda e: e.activation(u_b[:, 0:F], Xs, AF.Abs,
                                              scale=0.5),
                 reads=Xkey, writes=("u0", "u1"))
            p.op("sc", lambda e: e.activation(sg[:, 0:F], Xs, AF.Sign),
                 reads=Xkey, writes=(sgk,))

        def emit_wq_main(F, c0, sg, sgk):
            G = F // 16
            g0 = c0 // 16
            t1s = t1[:, c0:c0 + F]
            t2s = t2[:, c0:c0 + F]
            t1k = t_bands("t1", c0, F)
            t2k = t_bands("t2", c0, F)
            rbc = r16[:, g0:g0 + G].unsqueeze(2).broadcast_to([128, G, 16])
            ug = u_b[:, 0:F].rearrange("p (g k) -> p g k", k=16)
            p.op("ve", lambda e: e.tensor_tensor(ug, ug, rbc, ALU.mult),
                 reads=("u0", "u1", "r16"), writes=("u0", "u1"))
            uf = u_b[:, 0:F]
            p.op("ve", lambda e: e.tensor_scalar(t1s, uf, 5.0, 2.0,
                                                 ALU.is_gt, ALU.mult),
                 reads=("u0", "u1"), writes=t1k)
            chain = [(3.5, t1s, t2s, t1k, t2k),
                     (2.5, t2s, t1s, t2k, t1k),
                     (None, t1s, t2s, t1k, t2k),
                     (1.75, t2s, t1s, t2k, t1k),
                     (1.25, t1s, t2s, t1k, t2k),
                     (0.75, t2s, t1s, t2k, t1k),
                     (0.25, t1s, t2s, t1k, t2k)]
            for thr, sb, db, skk, dkk in chain:
                if thr is None:
                    p.op("ve", lambda e, sb=sb, db=db: e.tensor_scalar(
                        db, sb, 2.0, None, ALU.mult),
                        reads=skk, writes=dkk)
                else:
                    p.op("ve", lambda e, thr=thr, sb=sb, db=db:
                         e.scalar_tensor_tensor(db, uf, thr, sb, ALU.is_gt,
                                                ALU.add),
                         reads=("u0", "u1") + skk, writes=dkk)
            p.op("po", lambda e: e.tensor_tensor(t2s, t2s, sg[:, 0:F],
                                                 ALU.mult),
                 reads=t2k + (sgk,), writes=t2k)
            sbc = scl[:, g0:g0 + G].unsqueeze(2).broadcast_to([128, G, 16])
            t2g = t2s.rearrange("p (g k) -> p g k", k=16)
            p.op("po", lambda e: e.tensor_tensor(t2g, t2g, sbc, ALU.mult),
                 reads=t2k + ("scl",), writes=t2k)

        def stats_tile(par, idx, sq_ps, sq_ps_key):
            xt = x_t[par]
            xk = f"x{par}"
            p.op("ve", lambda e: e.reduce_sum(sm1[:, :], xt[:, :],
                                              axis=AX.X),
                 reads=(xk,), writes=("sm1",))
            p.op("ve", lambda e, idx=idx: e.tensor_scalar(
                mcol[:, idx:idx + 1], sm1[:, :], 1.0 / D, None, ALU.mult),
                reads=("sm1",), writes=("mcol",))
            p.op("ve", lambda e, idx=idx: e.tensor_scalar(
                sm2[:, :], mcol[:, idx:idx + 1], -1.0, None, ALU.mult),
                reads=("mcol",), writes=("sm2",))
            p.op("sc", lambda e: e.activation(sq_ps, xt[:, :], AF.Square,
                                              bias=sm2[:, 0:1], scale=1.0,
                                              accum_out=sm3[:, :]),
                 reads=(xk, "sm2"), writes=(sq_ps_key, "sm3"))
            p.op("ve", lambda e: e.tensor_scalar(sm3[:, :], sm3[:, :],
                                                 1.0 / D, LNEPS, ALU.mult,
                                                 ALU.add),
                 reads=("sm3",), writes=("sm3",))
            p.op("sc", lambda e: e.activation(sm4[:, :], sm3[:, :],
                                              AF.Sqrt),
                 reads=("sm3",), writes=("sm4",))
            p.op("ve", lambda e, idx=idx: e.reciprocal(
                rcol[:, idx:idx + 1], sm4[:, :]),
                reads=("sm4",), writes=("rcol",))

        def apply_ln(par, idx, to_f32):
            xt = x_t[par]
            xk = f"x{par}"
            if to_f32:
                p.op("ve", lambda e, idx=idx: e.tensor_scalar(
                    xn_f[:, :], xt[:, :], mcol[:, idx:idx + 1],
                    rcol[:, idx:idx + 1], ALU.subtract, ALU.mult),
                    reads=(xk, "mcol", "rcol"), writes=("xnf",))
                p.op("sc", lambda e: e.copy(xn_bf[:, :], xn_f[:, :]),
                     reads=("xnf",), writes=("xnbf",))
            else:
                p.op("ve", lambda e, idx=idx: e.tensor_scalar(
                    xn_bf[:, :], xt[:, :], mcol[:, idx:idx + 1],
                    rcol[:, idx:idx + 1], ALU.subtract, ALU.mult),
                    reads=(xk, "mcol", "rcol"), writes=("xnbf",))

        p.dma(idt[:, :], ident_d[:, :], writes=("idt",))
        p.op("ve", lambda e: e.memset(ones_col[:, :], 1.0),
             writes=("ones",))
        p.op("ve", lambda e: e.memset(ones_row[:, :], 1.0),
             writes=("ones",))

        with ExitStack() as sctx:
            S = sctx.enter_context
            rowf = S(nc.sbuf_tensor("rowf", [1, D], F32))
            ones_f32 = S(nc.sbuf_tensor("ones_f32", [1, 128], F32))
            bv8 = S(nc.sbuf_tensor("bv8", [8, 128], F32))
            bv8b = S(nc.sbuf_tensor("bv8b", [8, 128], BF16))
            g11 = S(nc.sbuf_tensor("g11", [1, 1], F32))
            bs11 = S(nc.sbuf_tensor("bs11", [1, 1], F32))
            bq_row = S(nc.sbuf_tensor("bq_row", [1, D], BF16))
            bk_row = S(nc.sbuf_tensor("bk_row", [1, D], BF16))
            spctx = S(ExitStack())
            ps_t8 = spctx.enter_context(nc.psum_tensor("ps_t8", [128, 8],
                                                       BF16))
            ps_c1 = spctx.enter_context(nc.psum_tensor("ps_c1", [128, 1],
                                                       F32))

            p.op("ve", lambda e: e.memset(ones_f32[:, :], 1.0),
                 writes=("ones",))
            p.dma(rowf[:, :], bq_d.ap().rearrange("(a n) -> a n", a=1),
                  writes=("rowf",))
            p.op("ve", lambda e: e.tensor_copy(bq_row[:, :], rowf[:, :]),
                 reads=("rowf",), writes=("bqr",))
            p.dma(rowf[:, :], bk_d.ap().rearrange("(a n) -> a n", a=1),
                  reads=("bqr",), writes=("rowf",))
            p.op("ve", lambda e: e.tensor_copy(bk_row[:, :], rowf[:, :]),
                 reads=("rowf",), writes=("bkr",))
            p.dma(rowf[:, :], bv_d.ap().rearrange("(a n) -> a n", a=1),
                  reads=("bkr",), writes=("rowf",))
            p.op("ve", lambda e: e.reduce_sum(bs11[:, :], rowf[:, :],
                                              axis=AX.X),
                 reads=("rowf",), writes=("bs11",))
            p.dma(bv8[:, :], bv_d.ap().rearrange("(a n) -> a n", a=8),
                  writes=("bv8",))
            p.op("ve", lambda e: e.tensor_copy(bv8b[:, :], bv8[:, :]),
                 reads=("bv8",), writes=("bv8b",))
            p.tr(ps_t8[:, :], bv8b[:, :], idt[0:8, 0:8],
                 reads=("bv8b", "idt"), writes=("pst8",))
            p.op("ve", lambda e: e.tensor_copy(bv_col[:, :], ps_t8[:, :]),
                 reads=("pst8",), writes=("bvcol",))
            p.dma(g11[:, :], gam_d.ap().rearrange("(a n) -> a n", a=1),
                  writes=("g11",))
            p.mm(ps_c1[:, :], ones_f32[:, :], g11[:, :], True, True,
                 reads=("ones", "g11"), writes=("psc1",))
            p.op("ve", lambda e: e.tensor_copy(gam_col[:, :], ps_c1[:, :]),
                 reads=("psc1",), writes=("gamc",))
            p.mm(ps_c1[:, :], ones_f32[:, :], bs11[:, :], True, True,
                 reads=("ones", "bs11", "gamc"), writes=("psc1",))
            p.op("ve", lambda e: e.tensor_copy(bvs_col[:, :], ps_c1[:, :]),
                 reads=("psc1",), writes=("bvsc",))
            p.play()
            spctx.close()

            with ExitStack() as actx:
                SA = actx.enter_context
                wv_b = SA(nc.sbuf_tensor("wv_b", [128, DC * D], BF16))
                A0 = SA(nc.sbuf_tensor("A0", [128, DC * D], F32))
                A1 = SA(nc.sbuf_tensor("A1", [128, DC * D], BF16))

                pctx = ExitStack()
                if True:
                    SP = pctx.enter_context
                    wqt_b = SP(nc.sbuf_tensor("wqt_b", [128, DC * D], BF16))
                    wkt_b = SP(nc.sbuf_tensor("wkt_b", [128, DC * D], BF16))
                    q0 = SP(nc.sbuf_tensor("q0", [128, 4 * D], BF16))
                    k0 = SP(nc.sbuf_tensor("k0", [128, 4 * D], BF16))
                    sqb = SP(nc.sbuf_tensor("sqb", [128, D], BF16))
                    wld = [SP(nc.sbuf_tensor("wld0", [128, D], F32))] * 2
                    sgn2 = SP(nc.sbuf_tensor("sgn2", [128, D], BF16))
                    sgl = [sgn, sgn2]
                    sgk_l = ["sgn", "sgnB"]
                    ps_tr = SP(nc.psum_tensor("ps_tr", [128, D], BF16))
                    ps_q = SP(nc.psum_tensor("ps_q", [128, D], F32))
                    ps_k = SP(nc.psum_tensor("ps_k", [128, D], F32))
                    ps_sm = SP(nc.psum_tensor("ps_sm", [128, 24], F32))
                    ps_a0 = [SP(nc.psum_tensor("ps_a00", [128, D], F32))]

                    for wi, (dst, src, dk) in enumerate((
                            (wqt_b, wqt_d, "wqtb"), (wkt_b, wkt_d, "wktb"),
                            (wv_b, wv_d, "wvb"))):
                        for dc in range(DC):
                            par = 0
                            wk = "wld0"
                            p.dma(wld[par][:, :],
                                  src[dc * 128:(dc + 1) * 128, :],
                                  writes=(wk,))
                            p.op("sc", lambda e, dst=dst, dc=dc, par=par:
                                 e.copy(dst[:, dc * D:(dc + 1) * D],
                                        wld[par][:, :]),
                                 reads=(wk,), writes=(dk,))
                    p.op("ve", lambda e: e.memset(A0[:, :], 0.0),
                         writes=("A0",))

                    first_small = [True]

                    def proj(ps_x, pskey, wt_b, wkey, b_row, bkey,
                             dst_bf, dkey, sq_col, qc_col, close):
                        for k in range(DC):
                            for hh in range(2):
                                p.mm(ps_x[:, hh * 512:(hh + 1) * 512],
                                     xnfm[:, k * 128:(k + 1) * 128],
                                     wt_b[:, k * D + hh * 512:
                                          k * D + (hh + 1) * 512],
                                     start=(k == 0), stop=False,
                                     reads=("xnfm", wkey), writes=(pskey,))
                        for hh in range(2):
                            p.mm(ps_x[:, hh * 512:(hh + 1) * 512],
                                 ones_row[:, :],
                                 b_row[:, hh * 512:(hh + 1) * 512],
                                 start=False, stop=True,
                                 reads=("ones", bkey), writes=(pskey,))
                        p.op("sc", lambda e, ps_x=ps_x: e.copy(
                            dst_bf, ps_x[:, :]),
                            reads=(pskey,), writes=(dkey,))
                        p.op("sc", lambda e, ps_x=ps_x: e.activation(
                            sqb[:, :], ps_x[:, :], AF.Square),
                            reads=(pskey,), writes=("sqb2",))
                        for k in range(DC):
                            st = first_small[0]
                            first_small[0] = False
                            p.mm(ps_sm[:, sq_col + k:sq_col + k + 1],
                                 sqb[:, k * 128:(k + 1) * 128],
                                 ones_col[:, :], start=st,
                                 stop=(close and qc_col is None
                                       and k == DC - 1),
                                 reads=("sqb2", "ones"), writes=("pssm",))
                        if qc_col is not None:
                            for k in range(DC):
                                p.mm(ps_sm[:, qc_col + k:qc_col + k + 1],
                                     dst_bf[:, k * 128:(k + 1) * 128],
                                     ones_col[:, :], start=False,
                                     stop=False,
                                     reads=(dkey, "ones"), writes=("pssm",))

                    def _wq_units():
                        units = []
                        jc = 0
                        for wsrc, wdst in ((wg_d, wgq_sp), (wu_d, wuq_sp)):
                            for ic in range(IC):
                                b = jc % 2
                                jc += 1
                                units.append(
                                    (wsrc[ic * 128:(ic + 1) * 128, :],
                                     b * 1024, D, D, b * 1024, b,
                                     wdst[ic * 128:(ic + 1) * 128, :],
                                     b * 1024, D))
                        for hc in range(DC):
                            wsrc = wd_d[hc * 128:(hc + 1) * 128, :]
                            wdst = wdq_sp[hc * 128:(hc + 1) * 128, :]
                            for pi, (w, c0) in enumerate(
                                    ((1024, 0), (1024, 1024), (768, 2048))):
                                units.append(
                                    (wsrc if pi == 0 else None, 0, I, w,
                                     c0, pi % 2,
                                     wdst if pi == 2 else None, 0, I))
                        units = [u if k < 62 else
                                 (u[0], u[1], u[2], u[3], u[4], 0, u[6],
                                  u[7], u[8])
                                 for k, u in enumerate(units)]
                        return units

                    _units = _wq_units()
                    _ucur = [0]
                    _upre = [0]

                    def pump_pre():
                        k = _upre[0]
                        if k >= len(_units):
                            return
                        src_, lc0, lF, w, c0, sb, dst, sc0, sF = _units[k]
                        if src_ is not None:
                            p.dma(Xb[:, lc0:lc0 + lF], src_,
                                  writes=xb_bands(lc0, lF))
                        emit_wq_pre(w, c0, sgl[sb], sgk_l[sb])
                        _upre[0] = k + 1

                    def pump_main():
                        k = _ucur[0]
                        if k >= len(_units) or k >= _upre[0]:
                            return
                        src_, lc0, lF, w, c0, sb, dst, sc0, sF = _units[k]
                        emit_wq_main(w, c0, sgl[sb], sgk_l[sb])
                        if dst is not None:
                            p.dma(dst, t2[:, sc0:sc0 + sF],
                                  reads=t_bands("t2", sc0, sF), queue="po")
                        _ucur[0] = k + 1

                    p.dma(x_t[0][:, :], xb_d[0:128, :], writes=("x0",))
                    stats_tile(0, 0, sqb[:, :], "sqb2")

                    for i in range(NCC):
                        cc = i % 4
                        par = i % 2
                        if i + 1 < NCC:
                            p.dma(x_t[(i + 1) % 2][:, :],
                                  xb_d[(i + 1) * 128:(i + 2) * 128, :],
                                  writes=(f"x{(i + 1) % 2}",))
                        apply_ln(par, i, to_f32=False)
                        for k in range(DC):
                            p.tr(ps_tr[:, k * 128:(k + 1) * 128],
                                 xn_bf[:, k * 128:(k + 1) * 128],
                                 idt[:, :],
                                 reads=("xnbf", "idt"), writes=("pstr",))
                        p.op("sc", lambda e: e.copy(xnfm[:, :],
                                                    ps_tr[:, :]),
                             reads=("pstr",), writes=("xnfm",))
                        if i >= 1:
                            pump_pre()
                        proj(ps_q, "psq", wqt_b, "wqtb", bq_row, "bqr",
                             q0[:, cc * D:(cc + 1) * D], f"q0{cc}", 0, 16,
                             close=False)
                        if i >= 1:
                            pump_main()
                            pump_pre()
                        proj(ps_k, "psk", wkt_b, "wktb", bk_row, "bkr",
                             k0[:, cc * D:(cc + 1) * D], f"k0{cc}", 8, None,
                             close=(i == NCC - 1))
                        if i >= 1:
                            pump_main()
                        if i + 1 < NCC:
                            stats_tile((i + 1) % 2, i + 1, sqb[:, :],
                                       "sqb2")
                        if cc == 3:
                            for dc in range(DC):
                                pa = 0
                                for c4 in range(4):
                                    for hh in range(2):
                                        p.mm(ps_a0[pa][:, hh * 512:
                                                       (hh + 1) * 512],
                                             k0[:, c4 * D + dc * 128:
                                                c4 * D + (dc + 1) * 128],
                                             q0[:, c4 * D + hh * 512:
                                                c4 * D + (hh + 1) * 512],
                                             start=(c4 == 0), stop=(c4 == 3),
                                             reads=(f"k0{c4}", f"q0{c4}"),
                                             writes=(f"psa{pa}",))
                                p.op("ve", lambda e, dc=dc, pa=pa:
                                     e.tensor_tensor(
                                         A0[:, dc * D:(dc + 1) * D],
                                         A0[:, dc * D:(dc + 1) * D],
                                         ps_a0[pa][:, :], ALU.add),
                                     reads=("A0", f"psa{pa}"),
                                     writes=("A0",))
                    p.op("ve", lambda e: e.tensor_copy(small[:, :],
                                                       ps_sm[:, :]),
                         reads=("pssm",), writes=("small",))
                    p.dma(cc_in_d[:, 0:DC * D], A0[:, :], reads=("A0",),
                          writes=("ccin",), queue="po")
                    p.dma(cc_in_d[:, DC * D:DC * D + 24], small[:, :],
                          reads=("small",), writes=("ccin",), queue="po")
                    p.op("po", lambda e: e.collective_compute(
                        "AllReduce", ALU.add, replica_groups=cc_groups,
                        ins=[cc_in_d.ap().opt()],
                        outs=[cc_out_d.ap().opt()]),
                        reads=("ccin",), writes=("ccout",))
                pctx.close()

                wgq = A(nc.sbuf_tensor("wgq", [128, DC * I], BF16,
                                       side="right"))
                wgqv = wgq[:, :].rearrange("p (s c) -> p s c", c=I)

                with ExitStack() as ectx:
                    SE = ectx.enter_context
                    rt = [SE(nc.sbuf_tensor(f"rt{i}", [128, D], BF16))
                          for i in (0, 1)]
                    ps_trE = SE(nc.psum_tensor("ps_trE", [128, D], BF16))

                    def reload_tile(spill, F, idx, tt, tk, dstv, tag, ps,
                                    pskey):
                        p.dma(tt[:, 0:F],
                              spill[idx * 128:(idx + 1) * 128, :],
                              writes=tk)
                        nsub = F // 128
                        for s0 in range(0, nsub, 8):
                            sn = min(8, nsub - s0)
                            for k in range(sn):
                                p.tr(ps[:, k * 128:(k + 1) * 128],
                                     tt[:, (s0 + k) * 128:(s0 + k + 1) * 128],
                                     idt[:, :],
                                     reads=tk + ("idt",), writes=(pskey,))
                            p.op("sc", lambda e, s0=s0, sn=sn, idx=idx,
                                 dstv=dstv, ps=ps: e.copy(
                                     dstv[:, s0:s0 + sn,
                                          idx * 128:(idx + 1) * 128],
                                     ps[:, 0:sn * 128].rearrange(
                                         "p (s c) -> p s c", c=128)),
                                 reads=(pskey,), writes=(tag,))

                    p.op("sy", lambda e: e.dma_start(
                        out=rt[0][:, :], in_=wgq_sp[0:128, :]),
                        reads=("_dmA",),
                        writes=("rt0", "wqtb", "wktb", "_dmA"),
                        inc=16, sem="sy")
                    nsub0 = D // 128
                    for k in range(nsub0):
                        p.tr(ps_trE[:, k * 128:(k + 1) * 128],
                             rt[0][:, k * 128:(k + 1) * 128], idt[:, :],
                             reads=("rt0", "idt"), writes=("pstrE",))
                    p.op("sc", lambda e: e.copy(
                        wgqv[:, 0:nsub0, 0:128],
                        ps_trE[:, 0:nsub0 * 128].rearrange(
                            "p (s c) -> p s c", c=128)),
                        reads=("pstrE",), writes=("wgq",))
                    for icx in range(1, IC):
                        reload_tile(wgq_sp, D, icx, rt[icx % 2],
                                    (f"rt{icx % 2}",), wgqv, "wgq",
                                    ps_trE, "pstrE")
                    while _ucur[0] < len(_units):
                        pump_pre()
                        pump_main()
                    p.dma(A0[:, :], cc_out_d[:, 0:DC * D],
                          reads=("ccout",), writes=("A0",), queue="po")
                    p.dma(small[:, :], cc_out_d[:, DC * D:DC * D + 24],
                          reads=("ccout",), writes=("small",), queue="po")
                    p.play()

                with ExitStack() as bctx:
                    SB = bctx.enter_context
                    nk_c = SB(nc.sbuf_tensor("nk_c", [128, 8], F32))
                    w1_c = SB(nc.sbuf_tensor("w1_c", [128, 8], F32))
                    bt_c = SB(nc.sbuf_tensor("bt_c", [128, 8], F32))
                    tl_c = SB(nc.sbuf_tensor("tl_c", [128, 8], F32))
                    sig_c = SB(nc.sbuf_tensor("sig_c", [128, 8], F32))
                    isg_c = SB(nc.sbuf_tensor("isg_c", [128, 8], F32))
                    dtmp = SB(nc.sbuf_tensor("dtmp", [128, 128], BF16))
                    colpk = SB(nc.sbuf_tensor("colpk", [128, 24], BF16))
                    rowuk = SB(nc.sbuf_tensor("rowuk", [24, 128], BF16))
                    ps_c8 = SB(nc.psum_tensor("ps_c8", [128, 8], F32))
                    ps_g1 = SB(nc.psum_tensor("ps_g1", [128, D], F32))
                    ps_t24 = SB(nc.psum_tensor("ps_t24", [24, 128], BF16))

                    p.op("sc", lambda e: e.activation(nq_c[:, :],
                                                      small[:, 0:8],
                                                      AF.Sqrt),
                         reads=("small",), writes=("nqc",))
                    p.op("ve", lambda e: e.reciprocal(inq_c[:, :],
                                                      nq_c[:, :]),
                         reads=("nqc",), writes=("inqc",))
                    p.op("sc", lambda e: e.activation(nk_c[:, :],
                                                      small[:, 8:16],
                                                      AF.Sqrt),
                         reads=("small",), writes=("nkc",))
                    p.op("ve", lambda e: e.reciprocal(ink_c[:, :],
                                                      nk_c[:, :]),
                         reads=("nkc",), writes=("inkc",))
                    for dc in range(DC):
                        p.op("ve", lambda e, dc=dc: e.tensor_scalar(
                            A1[:, dc * D:(dc + 1) * D],
                            A0[:, dc * D:(dc + 1) * D],
                            ink_c[:, dc:dc + 1], None, ALU.mult),
                            reads=("A0", "inkc"), writes=("A1",))
                    for nch in range(DC):
                        for dc in range(DC):
                            p.mm(ps_c8[:, nch:nch + 1],
                                 A1[:, dc * D + nch * 128:
                                    dc * D + (nch + 1) * 128],
                                 ones_col[:, :], start=(dc == 0),
                                 stop=(dc == 7),
                                 reads=("A1", "ones"), writes=("psc8",))
                    p.op("ve", lambda e: e.tensor_copy(w1_c[:, :],
                                                       ps_c8[:, :]),
                         reads=("psc8",), writes=("w1c",))
                    for nch in range(DC):
                        for dc in range(DC):
                            p.mm(ps_c8[:, nch:nch + 1],
                                 A1[:, dc * D + nch * 128:
                                    dc * D + (nch + 1) * 128],
                                 bv_col[:, dc:dc + 1], start=(dc == 0),
                                 stop=(dc == 7),
                                 reads=("A1", "bvcol", "w1c"),
                                 writes=("psc8",))
                    p.op("ve", lambda e: e.tensor_copy(bt_c[:, :],
                                                       ps_c8[:, :]),
                         reads=("psc8",), writes=("btc",))
                    for ec in range(DC):
                        for dc in range(DC):
                            p.mm(ps_c8[:, ec:ec + 1],
                                 wv_b[:, dc * D + ec * 128:
                                      dc * D + (ec + 1) * 128],
                                 ones_col[:, :], start=(dc == 0),
                                 stop=(dc == 7),
                                 reads=("wvb", "ones", "btc"),
                                 writes=("psc8",))
                    p.op("ve", lambda e: e.tensor_copy(wvs_c[:, :],
                                                       ps_c8[:, :]),
                         reads=("psc8",), writes=("wvsc",))
                    p.op("ve", lambda e: e.tensor_scalar(tl_c[:, :],
                                                         small[:, 16:24],
                                                         EPS, None,
                                                         ALU.mult),
                         reads=("small",), writes=("tlc",))
                    p.op("ve", lambda e: e.tensor_tensor(tl_c[:, :],
                                                         tl_c[:, :],
                                                         w1_c[:, :],
                                                         ALU.add),
                         reads=("tlc", "w1c"), writes=("tlc",))
                    p.op("ve", lambda e: e.tensor_tensor(tl_c[:, :],
                                                         tl_c[:, :],
                                                         inq_c[:, :],
                                                         ALU.mult),
                         reads=("tlc", "inqc"), writes=("tlc",))
                    p.op("ve", lambda e: e.tensor_scalar(tl_c[:, :],
                                                         tl_c[:, :],
                                                         float(D), None,
                                                         ALU.add),
                         reads=("tlc",), writes=("tlc",))
                    p.op("ve", lambda e: e.reciprocal(tl_c[:, :],
                                                      tl_c[:, :]),
                         reads=("tlc",), writes=("tlc",))
                    p.op("ve", lambda e: e.tensor_tensor(sig_c[:, :],
                                                         tl_c[:, :],
                                                         inq_c[:, :],
                                                         ALU.mult),
                         reads=("tlc", "inqc"), writes=("sigc",))
                    p.op("ve", lambda e: e.tensor_scalar(sig_c[:, :],
                                                         sig_c[:, :],
                                                         gam_col[:, 0:1],
                                                         None, ALU.mult),
                         reads=("sigc", "gamc"), writes=("sigc",))
                    p.op("ve", lambda e: e.tensor_copy(colpk[:, 0:8],
                                                       nq_c[:, :]),
                         reads=("nqc",), writes=("colpk",))
                    p.op("ve", lambda e: e.tensor_copy(colpk[:, 8:16],
                                                       bt_c[:, :]),
                         reads=("btc", "colpk"), writes=("colpk",))
                    p.op("ve", lambda e: e.tensor_copy(colpk[:, 16:24],
                                                       sig_c[:, :]),
                         reads=("sigc", "colpk"), writes=("colpk",))
                    p.tr(ps_t24[:, :], colpk[:, :], idt[:, :],
                         reads=("colpk", "idt"), writes=("pst24",))
                    p.op("ve", lambda e: e.tensor_copy(rowuk[:, :],
                                                       ps_t24[:, :]),
                         reads=("pst24",), writes=("rowuk",))
                    for dc in range(8):
                        p.dma(nq_row[0:1, dc * 128:(dc + 1) * 128],
                              rowuk[dc:dc + 1, :],
                              reads=("rowuk",), writes=("nqrow",))
                        p.dma(rs_row[0:1, dc * 128:(dc + 1) * 128],
                              rowuk[8 + dc:9 + dc, :],
                              reads=("rowuk",), writes=("rsrow",))
                        p.dma(sg_row[0:1, dc * 128:(dc + 1) * 128],
                              rowuk[16 + dc:17 + dc, :],
                              reads=("rowuk",), writes=("sgrow",))
                    for hh in range(2):
                        p.mm(ps_g1[:, hh * 512:(hh + 1) * 512],
                             ones_row[:, :],
                             sg_row[:, hh * 512:(hh + 1) * 512],
                             start=True, stop=True,
                             reads=("ones", "sgrow"), writes=("psg1",))
                    p.op("ve", lambda e: e.tensor_copy(sig_bc[:, :],
                                                       ps_g1[:, :]),
                         reads=("psg1",), writes=("sigbc",))
                    for ec in range(DC):
                        for dc in range(DC):
                            for hh in range(2):
                                p.mm(ps_g1[:, hh * 512:(hh + 1) * 512],
                                     wv_b[:, dc * D + ec * 128:
                                          dc * D + (ec + 1) * 128],
                                     A1[:, dc * D + hh * 512:
                                        dc * D + (hh + 1) * 512],
                                     start=(dc == 0), stop=(dc == 7),
                                     reads=("wvb", "A1", "sigbc"),
                                     writes=("psg1",))
                        p.op("ve", lambda e, ec=ec: e.tensor_copy(
                            g1t[:, ec * D:(ec + 1) * D], ps_g1[:, :]),
                            reads=("psg1",), writes=("g1t",))
                    p.op("ve", lambda e: e.reciprocal(isg_c[:, :],
                                                      sig_c[:, :]),
                         reads=("sigc",), writes=("isgc",))
                    for ec in range(DC):
                        p.op("ve", lambda e, ec=ec: e.tensor_scalar(
                            dtmp[:, :], idt[:, :], isg_c[:, ec:ec + 1],
                            None, ALU.mult),
                            reads=("idt", "isgc"), writes=("dtmp",))
                        p.op("ve", lambda e, ec=ec: e.tensor_tensor(
                            g1t[:, ec * D + ec * 128:
                                ec * D + (ec + 1) * 128],
                            g1t[:, ec * D + ec * 128:
                                ec * D + (ec + 1) * 128],
                            dtmp[:, :], ALU.add),
                            reads=("g1t", "dtmp"), writes=("g1t",))
                    p.play()

        with ExitStack() as cctx:
            SC = cctx.enter_context
            wuq = SC(nc.sbuf_tensor("wuq", [128, DC * I], BF16))
            wdq = SC(nc.sbuf_tensor("wdq", [128, IC * D], BF16))
            ps_trc = SC(nc.psum_tensor("ps_trc", [128, D], BF16))
            ps_g = SC(nc.psum_tensor("ps_g", [128, 512], F32))
            ps_u = SC(nc.psum_tensor("ps_u", [128, 512], F32))
            ps_dn = SC(nc.psum_tensor("ps_dn", [128, D], F32))
            ps_P = SC(nc.psum_tensor("ps_P", [128, D], F32))
            ps_vs = SC(nc.psum_tensor("ps_vs", [1, 128], F32))
            wuqv = wuq[:, :].rearrange("p (s c) -> p s c", c=I)
            wdqv = wdq[:, :].rearrange("p (s c) -> p s c", c=D)

            def reload2(spill, F, nrows, dstv, tag):
                for icx in range(nrows // 128):
                    parb = icx % 2
                    tkp = ("t1", "t2")[parb]
                    reload_tile(spill, F, icx, (t1, t2)[parb],
                                t_bands(tkp, 0, F), dstv, tag,
                                ps_trc, "pstrc")

            reload2(wuq_sp, D, I, wuqv, "wuq")
            reload2(wdq_sp, I, D, wdqv, "wdq")

            strips = [(0, 512), (512, 512), (1024, 512), (1536, 512),
                      (2048, 512), (2560, 256)]

            def emit_strip(i0, w, si):
                psu_s, psu_k = ((ps_u, "psu") if si % 2 == 0
                                else (ps_dn, "psdn"))
                for ec in range(DC):
                    p.mm(ps_g[:, 0:w], xqfm[:, ec * 128:(ec + 1) * 128],
                         wgq[:, ec * I + i0:ec * I + i0 + w],
                         start=(ec == 0), stop=(ec == 7),
                         reads=("xqfm", "wgq"), writes=("psg",))
                for ec in range(DC):
                    p.mm(psu_s[:, 0:w], xqfm[:, ec * 128:(ec + 1) * 128],
                         wuq[:, ec * I + i0:ec * I + i0 + w],
                         start=(ec == 0), stop=(ec == 7),
                         reads=("xqfm", "wuq"), writes=(psu_k,))
                bnd = xb_bands(i0, w)
                p.op("sc", lambda e, i0=i0, w=w: e.activation(
                    Xb[:, i0:i0 + w], ps_g[:, 0:w], AF.Silu),
                    reads=("psg",), writes=bnd)
                p.op("ve", lambda e, i0=i0, w=w, psu_s=psu_s:
                     e.tensor_tensor(Xb[:, i0:i0 + w], Xb[:, i0:i0 + w],
                                     psu_s[:, 0:w], ALU.mult),
                     reads=bnd + (psu_k,), writes=bnd)

            def emit_hq_tr(part):
                blocks = range(0, 11) if part == 0 else range(11, IC)
                t2k = (t_bands("t2", 0, 1408) if part == 0
                       else t_bands("t2", 1408, 1408))
                t1k = (t_bands("t1", 0, 1408) if part == 0
                       else t_bands("t1", 1408, 1408))
                blocks = list(blocks)
                for s0 in range(0, len(blocks), 8):
                    grp = blocks[s0:s0 + 8]
                    for k, blk in enumerate(grp):
                        p.tr(ps_trc[:, k * 128:(k + 1) * 128],
                             t2[:, blk * 128:(blk + 1) * 128], idt[:, :],
                             reads=t2k + ("idt",), writes=("pstrc",))
                    for k, blk in enumerate(grp):
                        p.op("sc", lambda e, k=k, blk=blk: e.copy(
                            t1[:, blk * 128:(blk + 1) * 128],
                            ps_trc[:, k * 128:(k + 1) * 128]),
                            reads=("pstrc",), writes=t1k)

            def emit_down_a(jj):
                for icc in range(DC):
                    for hh in range(2):
                        p.mm(ps_dn[:, hh * 512:(hh + 1) * 512],
                             t1[:, icc * 128:(icc + 1) * 128],
                             wdq[:, icc * D + hh * 512:
                                 icc * D + (hh + 1) * 512],
                             start=(icc == 0), stop=False,
                             reads=t_bands("t1", 0, 1024) + ("wdq",),
                             writes=("psdn",))

            def emit_down_tail(jj, parj):
                for icc in range(DC, IC):
                    tk = t_bands("t1", icc * 128, 128)
                    for hh in range(2):
                        p.mm(ps_dn[:, hh * 512:(hh + 1) * 512],
                             t1[:, icc * 128:(icc + 1) * 128],
                             wdq[:, icc * D + hh * 512:
                                 icc * D + (hh + 1) * 512],
                             start=False, stop=(icc == IC - 1),
                             reads=tk + ("wdq",), writes=("psdn",))
                p.op("sc", lambda e: e.activation(g_bf[:, :],
                                                  ps_dn[:, :], AF.Silu),
                     reads=("psdn",), writes=("gbf",))
                p.op("po", lambda e: e.tensor_tensor(Xb[:, D:2 * D],
                                                     Xb[:, 0:D],
                                                     g_bf[:, :],
                                                     ALU.mult),
                     reads=("Xb0", "gbf"), writes=("Xb1",))
                p.op("po", lambda e, parj=parj: e.tensor_tensor(
                    Xb[:, D:2 * D], Xb[:, D:2 * D], x_t[parj][:, :],
                    ALU.add),
                    reads=(f"x{parj}",), writes=("Xb1",))
                p.dma(out_d[jj * 128:(jj + 1) * 128, :], Xb[:, D:2 * D],
                      reads=("Xb1",), writes=("OUT",), queue="po")

            p.dma(x_t[0][:, :], xb_d[0:128, :], writes=("x0",))
            for j in range(NCC):
                par = j % 2
                apply_ln(par, j, to_f32=True)
                if j > 0:
                    emit_down_a(j - 1)
                emit_quant_half(xn_f, "xnf", D, 0, filler=True)
                if j > 0:
                    emit_hq_tr(1)
                    emit_down_tail(j - 1, (j - 1) % 2)
                if j + 1 < NCC:
                    p.dma(x_t[(j + 1) % 2][:, :],
                          xb_d[(j + 1) * 128:(j + 2) * 128, :],
                          writes=(f"x{(j + 1) % 2}",))
                for k in range(DC):
                    p.tr(ps_trc[:, k * 128:(k + 1) * 128],
                         xn_bf[:, k * 128:(k + 1) * 128], idt[:, :],
                         reads=("xnbf", "idt"), writes=("pstrc",))
                p.op("sc", lambda e: e.copy(xnfm[:, :], ps_trc[:, :]),
                     reads=("pstrc",), writes=("xnfm",))
                for k in range(DC):
                    p.tr(ps_trc[:, k * 128:(k + 1) * 128],
                         t2[:, k * 128:(k + 1) * 128], idt[:, :],
                         reads=t_bands("t2", 0, 1024) + ("idt",),
                         writes=("pstrc",))
                p.op("sc", lambda e: e.copy(xqfm[:, :], ps_trc[:, :]),
                     reads=("pstrc",), writes=("xqfm",))
                for si, (i0, w) in enumerate(strips):
                    emit_strip(i0, w, si)
                    if si == 1:
                        emit_quant_half(Xb, xb_bands(0, 1024), 1024, 0,
                                        filler=True)
                    elif si == 3:
                        emit_quant_half(Xb, xb_bands(1024, 1024), 1024,
                                        1024, filler=True)
                emit_quant_half(Xb, xb_bands(2048, 768), 768, 2048,
                                filler=True)
                for ec in range(DC):
                    p.mm(ps_vs[:, :], wvs_c[:, ec:ec + 1],
                         xnfm[:, ec * 128:(ec + 1) * 128],
                         start=(ec == 0), stop=(ec == 7),
                         reads=("wvsc", "xnfm"), writes=("psvs",))
                p.op("ve", lambda e: e.tensor_scalar(
                    vs_row[:, :], ps_vs[:, :], bvs_col[0:1, 0:1], None,
                    ALU.add),
                    reads=("psvs", "bvsc"), writes=("vsrow",))
                for ec in range(DC):
                    for hh in range(2):
                        p.mm(ps_P[:, hh * 512:(hh + 1) * 512],
                             xnfm[:, ec * 128:(ec + 1) * 128],
                             g1t[:, ec * D + hh * 512:
                                 ec * D + (hh + 1) * 512],
                             start=(ec == 0), stop=False,
                             reads=("xnfm", "g1t"), writes=("psP",))
                for hh in range(2):
                    p.mm(ps_P[:, hh * 512:(hh + 1) * 512], vs_row[:, :],
                         nq_row[:, hh * 512:(hh + 1) * 512], start=False,
                         stop=False,
                         reads=("vsrow", "nqrow"), writes=("psP",))
                for hh in range(2):
                    p.mm(ps_P[:, hh * 512:(hh + 1) * 512], ones_row[:, :],
                         rs_row[:, hh * 512:(hh + 1) * 512], start=False,
                         stop=True,
                         reads=("ones", "rsrow"), writes=("psP",))
                emit_hq_tr(0)
                p.op("ve", lambda e: e.tensor_tensor(Xb[:, 0:D],
                                                     sig_bc[:, :],
                                                     ps_P[:, :], ALU.mult),
                     reads=("sigbc", "psP"), writes=("Xb0",))
            emit_down_a(NCC - 1)
            emit_hq_tr(1)
            emit_down_tail(NCC - 1, (NCC - 1) % 2)
            p.play()
        p.finish()

    return nc


_NC_CACHE = {}


def kernel(**inputs):
    x = np.ascontiguousarray(np.asarray(inputs["L0_attn_x"],
                                        dtype=np.float32))
    wq = np.asarray(inputs["L0_attn_Wq"], dtype=np.float32)
    wk = np.asarray(inputs["L0_attn_Wk"], dtype=np.float32)
    wv = np.ascontiguousarray(np.asarray(inputs["L0_attn_Wv"],
                                         dtype=np.float32))
    bq = np.ascontiguousarray(np.asarray(inputs["L0_attn_bq"],
                                         dtype=np.float32))
    bk = np.ascontiguousarray(np.asarray(inputs["L0_attn_bk"],
                                         dtype=np.float32))
    bv = np.ascontiguousarray(np.asarray(inputs["L0_attn_bv"],
                                         dtype=np.float32))
    gam = np.ascontiguousarray(np.asarray(inputs["L0_attn_gamma"],
                                          dtype=np.float32))
    wg = np.ascontiguousarray(np.asarray(inputs["L0_ffn_w_gate"],
                                         dtype=np.float32))
    wu = np.ascontiguousarray(np.asarray(inputs["L0_ffn_w_up"],
                                         dtype=np.float32))
    wd = np.ascontiguousarray(np.asarray(inputs["L0_ffn_w_down"],
                                         dtype=np.float32))
    wqt = np.ascontiguousarray(wq.T)
    wkt = np.ascontiguousarray(wk.T)
    ident = np.eye(128, dtype=ml_dtypes.bfloat16)

    if "nc" not in _NC_CACHE:
        _NC_CACHE["nc"] = build()
    nc = _NC_CACHE["nc"]

    in_maps = []
    for c in range(8):
        b, h = c // 2, c % 2
        if h == 0:
            xb = np.ascontiguousarray(x[b])
        else:
            xb = np.ascontiguousarray(
                np.concatenate([x[b, TOKC:], x[b, :TOKC]], axis=0))
        in_maps.append({
            "xb": xb,
            "wqt": wqt, "wkt": wkt, "wv": wv,
            "bq": bq, "bk": bk, "bv": bv, "gam": gam,
            "wg": wg, "wu": wu, "wd": wd, "ident": ident,
        })
    res = run_bass_kernel_spmd(nc, in_maps, core_ids=list(range(8)))
    out = np.empty((4, TOKA, D), dtype=np.float32)
    for c in range(8):
        b, h = c // 2, c % 2
        out[b, h * TOKC:(h + 1) * TOKC] = res.results[c]["out"]
    return out


# version 27
# speedup vs baseline: 1.0228x; 1.0228x over previous
import numpy as np
import ml_dtypes
import concourse.bass as bass
import concourse.mybir as mybir
from concourse.bass_utils import run_bass_kernel_spmd

F32 = mybir.dt.float32
BF16 = mybir.dt.bfloat16
AF = mybir.ActivationFunctionType
ALU = mybir.AluOpType
AX = mybir.AxisListType

D = 1024
I = 2816
TOKA = 8192
TOKC = 4096
NCA = TOKA // 128
NCC = TOKC // 128
DC = 8
IC = 22
EPS = 1e-6
LNEPS = 1e-5

ENGS = ("sy", "te", "ve", "sc", "po")
SEMS = ENGS + ("pd",)


class Op:
    __slots__ = ("ek", "sem", "fn", "inc", "need_inc", "semval", "waits")

    def __init__(self, ek, sem, fn, inc):
        self.ek, self.sem, self.fn, self.inc = ek, sem, fn, inc
        self.need_inc = inc == 16
        self.semval = None
        self.waits = []


class Prog:

    def __init__(self, nc):
        self.nc = nc
        self.sems = {e: nc.ctx.enter_context(nc.semaphore(f"s_{e}"))
                     for e in SEMS}
        self.ops = []
        self.counts = {e: 0 for e in SEMS}
        self.last_write = {}
        self.readers = {}
        self.fences = {}

    def op(self, ek, fn, reads=(), writes=(), inc=1, sem=None):
        ps_reads = tuple(k for k in reads if k.startswith("ps"))
        if ps_reads:
            reads = tuple(k for k in reads if not k.startswith("ps"))
            writes = tuple(writes) + ps_reads
        o = Op(ek, sem or ek, fn, inc)
        deps = []
        for k in reads:
            w = self.last_write.get(k)
            if w is not None:
                deps.append(w)
        for k in writes:
            w = self.last_write.get(k)
            if w is not None:
                deps.append(w)
            deps.extend(self.readers.get(k, {}).values())
        for d in deps:
            if d is o:
                continue
            if d.ek == "te" and ek == "te":
                continue
            if d not in o.waits:
                o.waits.append(d)
                d.need_inc = True
        for k in reads:
            self.readers.setdefault(k, {})[ek] = o
        for k in writes:
            self.last_write[k] = o
            self.readers[k] = {}
        self.ops.append(o)
        return o

    def dma(self, out, in_, reads=(), writes=(), queue="sy"):
        if queue == "sy":
            chain, sem = "_dmA", "sy"
        else:
            chain, sem = "_dmB", "pd"
        return self.op(queue, lambda e: e.dma_start(out=out, in_=in_),
                       reads=tuple(reads) + (chain,),
                       writes=tuple(writes) + (chain,), inc=16, sem=sem)

    def mm(self, out, lhsT, rhs, start, stop, reads=(), writes=()):
        return self.op("te", lambda e: e.matmul(out, lhsT, rhs, start=start,
                                                stop=stop),
                       reads=reads, writes=writes)

    def tr(self, out, in_, ident, reads=(), writes=()):
        return self.op("te", lambda e: e.transpose(out, in_, ident),
                       reads=reads, writes=writes)

    def play(self):
        ops, self.ops = self.ops, []
        if not ops:
            return
        lastop = {}
        for o in ops:
            lastop[o.sem] = o
        for o in lastop.values():
            o.need_inc = True
        for o in ops:
            if o.need_inc:
                self.counts[o.sem] += o.inc
                o.semval = self.counts[o.sem]
        with self.nc.Block() as block:
            sections = {"sy": block.sync, "te": block.tensor,
                        "ve": block.vector, "sc": block.scalar,
                        "po": block.gpsimd}
            for ek, section in sections.items():
                mine = [o for o in ops if o.ek == ek]
                if not mine:
                    continue
                sems = self.sems

                def body(eng, mine=mine, sems=sems):
                    water = {e: 0 for e in SEMS}
                    for o in mine:
                        for d in o.waits:
                            v = d.semval
                            assert v is not None
                            if v > water[d.sem]:
                                eng.wait_ge(sems[d.sem], v)
                                water[d.sem] = v
                        ins = o.fn(eng)
                        if o.need_inc:
                            ins.then_inc(sems[o.sem], o.inc)

                section(body)
        fence = dict(lastop)
        for k in list(self.last_write):
            w = self.last_write[k]
            self.last_write[k] = fence.get(w.sem, w)
        for k in list(self.readers):
            self.readers[k] = {e: fence.get(o.sem, o)
                               for e, o in self.readers[k].items()}

    def finish(self):
        counts = dict(self.counts)
        with self.nc.Block() as block:
            def body(eng):
                for e in SEMS:
                    if counts[e]:
                        eng.wait_ge(self.sems[e], counts[e])
            block.sync(body)


def t_bands(pfx, c0, w):
    bands = []
    for b, lo, hi in ((pfx + "a", 0, 1024), (pfx + "b", 1024, 2048),
                      (pfx + "c", 2048, 2816)):
        if c0 < hi and c0 + w > lo:
            bands.append(b)
    return tuple(bands)


def xb_bands(c0, w):
    bands = []
    for b, lo, hi in (("Xb0", 0, 1024), ("Xb1", 1024, 2048),
                      ("Xb2b", 2048, 2816)):
        if c0 < hi and c0 + w > lo:
            bands.append(b)
    return tuple(bands)


def build(cc_groups=((0, 1), (2, 3), (4, 5), (6, 7))):
    cc_groups = [list(g) for g in cc_groups]
    nc = bass.Bass("TRN2", target_bir_lowering=False)

    xb_d = nc.dram_tensor("xb", [TOKA, D], F32, kind="ExternalInput")
    wqt_d = nc.dram_tensor("wqt", [D, D], F32, kind="ExternalInput")
    wkt_d = nc.dram_tensor("wkt", [D, D], F32, kind="ExternalInput")
    wv_d = nc.dram_tensor("wv", [D, D], F32, kind="ExternalInput")
    bq_d = nc.dram_tensor("bq", [D], F32, kind="ExternalInput")
    bk_d = nc.dram_tensor("bk", [D], F32, kind="ExternalInput")
    bv_d = nc.dram_tensor("bv", [D], F32, kind="ExternalInput")
    gam_d = nc.dram_tensor("gam", [1], F32, kind="ExternalInput")
    wg_d = nc.dram_tensor("wg", [I, D], F32, kind="ExternalInput")
    wu_d = nc.dram_tensor("wu", [I, D], F32, kind="ExternalInput")
    wd_d = nc.dram_tensor("wd", [D, I], F32, kind="ExternalInput")
    ident_d = nc.dram_tensor("ident", [128, 128], BF16, kind="ExternalInput")
    out_d = nc.dram_tensor("out", [TOKC, D], F32, kind="ExternalOutput")
    wgq_sp = nc.dram_tensor("wgq_sp", [I, D], BF16, kind="Internal")
    wuq_sp = nc.dram_tensor("wuq_sp", [I, D], BF16, kind="Internal")
    wdq_sp = nc.dram_tensor("wdq_sp", [D, I], BF16, kind="Internal")
    cc_in_d = nc.dram_tensor("cc_in", [128, DC * D + 24], F32,
                             kind="Internal")
    cc_out_d = nc.dram_tensor("cc_out", [128, DC * D + 24], F32,
                              kind="Internal")

    from contextlib import ExitStack
    with ExitStack() as ctx:
        A = ctx.enter_context

        x_t = [A(nc.sbuf_tensor(f"x_t{i}", [128, D], F32)) for i in (0, 1)]
        xn_f = A(nc.sbuf_tensor("xn_f", [128, D], F32))
        xn_bf = A(nc.sbuf_tensor("xn_bf", [128, D], BF16))
        xnfm = A(nc.sbuf_tensor("xnfm", [128, D], BF16))
        xqfm = A(nc.sbuf_tensor("xqfm", [128, D], BF16))
        Xb = A(nc.sbuf_tensor("Xb", [128, I], F32))
        u_b = A(nc.sbuf_tensor("u_b", [128, D], F32))
        t1 = A(nc.sbuf_tensor("t1", [128, I], BF16))
        t2 = A(nc.sbuf_tensor("t2", [128, I], BF16))
        sgn = A(nc.sbuf_tensor("sgn", [128, D], BF16))
        g_bf = A(nc.sbuf_tensor("g_bf", [128, D], BF16))
        r16 = A(nc.sbuf_tensor("r16", [128, I // 16], F32))
        scl = A(nc.sbuf_tensor("scl", [128, I // 16], F32))
        mcol = A(nc.sbuf_tensor("mcol", [128, NCC], F32))
        rcol = A(nc.sbuf_tensor("rcol", [128, NCC], F32))
        sm1 = A(nc.sbuf_tensor("sm1", [128, 1], F32))
        sm2 = A(nc.sbuf_tensor("sm2", [128, 1], F32))
        sm3 = A(nc.sbuf_tensor("sm3", [128, 1], F32))
        sm4 = A(nc.sbuf_tensor("sm4", [128, 1], F32))
        gam_col = A(nc.sbuf_tensor("gam_col", [128, 1], F32))
        bvs_col = A(nc.sbuf_tensor("bvs_col", [128, 1], F32))
        idt = A(nc.sbuf_tensor("idt", [128, 128], BF16))
        ones_col = A(nc.sbuf_tensor("ones_col", [128, 1], BF16))
        ones_row = A(nc.sbuf_tensor("ones_row", [1, 128], BF16))
        bv_col = A(nc.sbuf_tensor("bv_col", [128, 8], BF16))
        small = A(nc.sbuf_tensor("small", [128, 24], F32))
        nq_c = A(nc.sbuf_tensor("nq_c", [128, 8], F32))
        inq_c = A(nc.sbuf_tensor("inq_c", [128, 8], F32))
        ink_c = A(nc.sbuf_tensor("ink_c", [128, 8], F32))
        wvs_c = A(nc.sbuf_tensor("wvs_c", [128, 8], BF16))
        nq_row = A(nc.sbuf_tensor("nq_row", [1, D], BF16))
        rs_row = A(nc.sbuf_tensor("rs_row", [1, D], BF16))
        sg_row = A(nc.sbuf_tensor("sg_row", [1, D], BF16))
        sig_bc = A(nc.sbuf_tensor("sig_bc", [128, D], BF16))
        g1t = A(nc.sbuf_tensor("g1t", [128, DC * D], BF16))
        vs_row = A(nc.sbuf_tensor("vs_row", [1, 128], BF16))

        p = Prog(nc)

        def emit_quant_half(X, Xkey, F, c0, filler=False):
            if isinstance(Xkey, str):
                Xkey = (Xkey,)
            G = F // 16
            g0 = c0 // 16
            Xs = X[:, c0:c0 + F]
            Xg = Xs.rearrange("p (g k) -> p g k", k=16)
            t1s = t1[:, c0:c0 + F]
            t2s = t2[:, c0:c0 + F]
            t1k = t_bands("t1", c0, F)
            t2k = t_bands("t2", c0, F)
            ukeys = ("u0",) if F <= 512 else ("u0", "u1")
            e_mul = "ve"
            p.op("ve", lambda e: e.reduce_max(r16[:, g0:g0 + G], Xg,
                                              axis=AX.X,
                                              apply_absolute_value=True),
                 reads=Xkey, writes=("r16",))
            p.op("ve", lambda e: e.tensor_scalar(scl[:, g0:g0 + G],
                                                 r16[:, g0:g0 + G],
                                                 1.0 / 12.0, 5e-13,
                                                 ALU.mult, ALU.max),
                 reads=("r16",), writes=("scl",))
            p.op("ve", lambda e: e.reciprocal(r16[:, g0:g0 + G],
                                              scl[:, g0:g0 + G]),
                 reads=("scl",), writes=("r16",))
            p.op("sc", lambda e: e.activation(u_b[:, 0:F], Xs, AF.Abs,
                                              scale=0.5),
                 reads=Xkey, writes=ukeys)
            p.op("sc", lambda e: e.activation(sgn[:, 0:F], Xs, AF.Sign),
                 reads=Xkey, writes=("sgn",))
            rbc = r16[:, g0:g0 + G].unsqueeze(2).broadcast_to([128, G, 16])
            ug = u_b[:, 0:F].rearrange("p (g k) -> p g k", k=16)
            p.op(e_mul, lambda e: e.tensor_tensor(ug, ug, rbc, ALU.mult),
                 reads=ukeys + ("r16",), writes=ukeys)
            uf = u_b[:, 0:F]
            p.op(e_mul, lambda e: e.tensor_scalar(t1s, uf, 5.0, 2.0,
                                                  ALU.is_gt, ALU.mult),
                 reads=ukeys, writes=t1k)
            chain = [(3.5, t1s, t2s, t1k, t2k), (2.5, t2s, t1s, t2k, t1k),
                     (None, t1s, t2s, t1k, t2k),
                     (1.75, t2s, t1s, t2k, t1k),
                     (1.25, t1s, t2s, t1k, t2k),
                     (0.75, t2s, t1s, t2k, t1k),
                     (0.25, t1s, t2s, t1k, t2k)]
            for thr, sb, db, skk, dkk in chain:
                if thr is None:
                    p.op("ve", lambda e, sb=sb, db=db: e.tensor_scalar(
                        db, sb, 2.0, None, ALU.mult),
                        reads=skk, writes=dkk)
                else:
                    p.op("ve", lambda e, thr=thr, sb=sb, db=db:
                         e.scalar_tensor_tensor(db, uf, thr, sb, ALU.is_gt,
                                                ALU.add),
                         reads=ukeys + skk, writes=dkk)
            t2g = t2s.rearrange("p (g k) -> p g k", k=16)
            e_fin = "ve"
            p.op(e_fin, lambda e: e.tensor_tensor(t2s, t2s, sgn[:, 0:F],
                                                  ALU.mult),
                 reads=t2k + ("sgn",), writes=t2k)
            sbc = scl[:, g0:g0 + G].unsqueeze(2).broadcast_to([128, G, 16])
            p.op(e_fin, lambda e: e.tensor_tensor(t2g, t2g, sbc, ALU.mult),
                 reads=t2k + ("scl",), writes=t2k)

        def emit_wq_pre(F, c0, sg, sgk):
            G = F // 16
            g0 = c0 // 16
            Xkey = xb_bands(c0, F)
            Xs = Xb[:, c0:c0 + F]
            Xg = Xs.rearrange("p (g k) -> p g k", k=16)
            p.op("ve", lambda e: e.reduce_max(r16[:, g0:g0 + G], Xg,
                                              axis=AX.X,
                                              apply_absolute_value=True),
                 reads=Xkey, writes=("r16",))
            p.op("ve", lambda e: e.tensor_scalar(scl[:, g0:g0 + G],
                                                 r16[:, g0:g0 + G],
                                                 1.0 / 12.0, 5e-13,
                                                 ALU.mult, ALU.max),
                 reads=("r16",), writes=("scl",))
            p.op("ve", lambda e: e.reciprocal(r16[:, g0:g0 + G],
                                              scl[:, g0:g0 + G]),
                 reads=("scl",), writes=("r16",))
            p.op("sc", lambda e: e.activation(u_b[:, 0:F], Xs, AF.Abs,
                                              scale=0.5),
                 reads=Xkey, writes=("u0", "u1"))
            p.op("sc", lambda e: e.activation(sg[:, 0:F], Xs, AF.Sign),
                 reads=Xkey, writes=(sgk,))

        def emit_wq_main(F, c0, sg, sgk):
            G = F // 16
            g0 = c0 // 16
            t1s = t1[:, c0:c0 + F]
            t2s = t2[:, c0:c0 + F]
            t1k = t_bands("t1", c0, F)
            t2k = t_bands("t2", c0, F)
            rbc = r16[:, g0:g0 + G].unsqueeze(2).broadcast_to([128, G, 16])
            ug = u_b[:, 0:F].rearrange("p (g k) -> p g k", k=16)
            p.op("ve", lambda e: e.tensor_tensor(ug, ug, rbc, ALU.mult),
                 reads=("u0", "u1", "r16"), writes=("u0", "u1"))
            uf = u_b[:, 0:F]
            p.op("ve", lambda e: e.tensor_scalar(t1s, uf, 5.0, 2.0,
                                                 ALU.is_gt, ALU.mult),
                 reads=("u0", "u1"), writes=t1k)
            chain = [(3.5, t1s, t2s, t1k, t2k),
                     (2.5, t2s, t1s, t2k, t1k),
                     (None, t1s, t2s, t1k, t2k),
                     (1.75, t2s, t1s, t2k, t1k),
                     (1.25, t1s, t2s, t1k, t2k),
                     (0.75, t2s, t1s, t2k, t1k),
                     (0.25, t1s, t2s, t1k, t2k)]
            for thr, sb, db, skk, dkk in chain:
                if thr is None:
                    p.op("ve", lambda e, sb=sb, db=db: e.tensor_scalar(
                        db, sb, 2.0, None, ALU.mult),
                        reads=skk, writes=dkk)
                else:
                    p.op("ve", lambda e, thr=thr, sb=sb, db=db:
                         e.scalar_tensor_tensor(db, uf, thr, sb, ALU.is_gt,
                                                ALU.add),
                         reads=("u0", "u1") + skk, writes=dkk)
            p.op("po", lambda e: e.tensor_tensor(t2s, t2s, sg[:, 0:F],
                                                 ALU.mult),
                 reads=t2k + (sgk,), writes=t2k)
            sbc = scl[:, g0:g0 + G].unsqueeze(2).broadcast_to([128, G, 16])
            t2g = t2s.rearrange("p (g k) -> p g k", k=16)
            p.op("po", lambda e: e.tensor_tensor(t2g, t2g, sbc, ALU.mult),
                 reads=t2k + ("scl",), writes=t2k)

        def stats_tile(par, idx, sq_ps, sq_ps_key):
            xt = x_t[par]
            xk = f"x{par}"
            p.op("ve", lambda e: e.reduce_sum(sm1[:, :], xt[:, :],
                                              axis=AX.X),
                 reads=(xk,), writes=("sm1",))
            p.op("ve", lambda e, idx=idx: e.tensor_scalar(
                mcol[:, idx:idx + 1], sm1[:, :], 1.0 / D, None, ALU.mult),
                reads=("sm1",), writes=("mcol",))
            p.op("ve", lambda e, idx=idx: e.tensor_scalar(
                sm2[:, :], mcol[:, idx:idx + 1], -1.0, None, ALU.mult),
                reads=("mcol",), writes=("sm2",))
            p.op("sc", lambda e: e.activation(sq_ps, xt[:, :], AF.Square,
                                              bias=sm2[:, 0:1], scale=1.0,
                                              accum_out=sm3[:, :]),
                 reads=(xk, "sm2"), writes=(sq_ps_key, "sm3"))
            p.op("ve", lambda e: e.tensor_scalar(sm3[:, :], sm3[:, :],
                                                 1.0 / D, LNEPS, ALU.mult,
                                                 ALU.add),
                 reads=("sm3",), writes=("sm3",))
            p.op("sc", lambda e: e.activation(sm4[:, :], sm3[:, :],
                                              AF.Sqrt),
                 reads=("sm3",), writes=("sm4",))
            p.op("ve", lambda e, idx=idx: e.reciprocal(
                rcol[:, idx:idx + 1], sm4[:, :]),
                reads=("sm4",), writes=("rcol",))

        def apply_ln(par, idx, to_f32):
            xt = x_t[par]
            xk = f"x{par}"
            if to_f32:
                p.op("ve", lambda e, idx=idx: e.tensor_scalar(
                    xn_f[:, :], xt[:, :], mcol[:, idx:idx + 1],
                    rcol[:, idx:idx + 1], ALU.subtract, ALU.mult),
                    reads=(xk, "mcol", "rcol"), writes=("xnf",))
                p.op("sc", lambda e: e.copy(xn_bf[:, :], xn_f[:, :]),
                     reads=("xnf",), writes=("xnbf",))
            else:
                p.op("ve", lambda e, idx=idx: e.tensor_scalar(
                    xn_bf[:, :], xt[:, :], mcol[:, idx:idx + 1],
                    rcol[:, idx:idx + 1], ALU.subtract, ALU.mult),
                    reads=(xk, "mcol", "rcol"), writes=("xnbf",))

        p.dma(idt[:, :], ident_d[:, :], writes=("idt",))
        p.op("ve", lambda e: e.memset(ones_col[:, :], 1.0),
             writes=("ones",))
        p.op("ve", lambda e: e.memset(ones_row[:, :], 1.0),
             writes=("ones",))

        with ExitStack() as sctx:
            S = sctx.enter_context
            rowf = S(nc.sbuf_tensor("rowf", [1, D], F32))
            ones_f32 = S(nc.sbuf_tensor("ones_f32", [1, 128], F32))
            bv8 = S(nc.sbuf_tensor("bv8", [8, 128], F32))
            bv8b = S(nc.sbuf_tensor("bv8b", [8, 128], BF16))
            g11 = S(nc.sbuf_tensor("g11", [1, 1], F32))
            bs11 = S(nc.sbuf_tensor("bs11", [1, 1], F32))
            bq_row = S(nc.sbuf_tensor("bq_row", [1, D], BF16))
            bk_row = S(nc.sbuf_tensor("bk_row", [1, D], BF16))
            spctx = S(ExitStack())
            ps_t8 = spctx.enter_context(nc.psum_tensor("ps_t8", [128, 8],
                                                       BF16))
            ps_c1 = spctx.enter_context(nc.psum_tensor("ps_c1", [128, 1],
                                                       F32))

            p.op("ve", lambda e: e.memset(ones_f32[:, :], 1.0),
                 writes=("ones",))
            p.dma(rowf[:, :], bq_d.ap().rearrange("(a n) -> a n", a=1),
                  writes=("rowf",))
            p.op("ve", lambda e: e.tensor_copy(bq_row[:, :], rowf[:, :]),
                 reads=("rowf",), writes=("bqr",))
            p.dma(rowf[:, :], bk_d.ap().rearrange("(a n) -> a n", a=1),
                  reads=("bqr",), writes=("rowf",))
            p.op("ve", lambda e: e.tensor_copy(bk_row[:, :], rowf[:, :]),
                 reads=("rowf",), writes=("bkr",))
            p.dma(rowf[:, :], bv_d.ap().rearrange("(a n) -> a n", a=1),
                  reads=("bkr",), writes=("rowf",))
            p.op("ve", lambda e: e.reduce_sum(bs11[:, :], rowf[:, :],
                                              axis=AX.X),
                 reads=("rowf",), writes=("bs11",))
            p.dma(bv8[:, :], bv_d.ap().rearrange("(a n) -> a n", a=8),
                  writes=("bv8",))
            p.op("ve", lambda e: e.tensor_copy(bv8b[:, :], bv8[:, :]),
                 reads=("bv8",), writes=("bv8b",))
            p.tr(ps_t8[:, :], bv8b[:, :], idt[0:8, 0:8],
                 reads=("bv8b", "idt"), writes=("pst8",))
            p.op("ve", lambda e: e.tensor_copy(bv_col[:, :], ps_t8[:, :]),
                 reads=("pst8",), writes=("bvcol",))
            p.dma(g11[:, :], gam_d.ap().rearrange("(a n) -> a n", a=1),
                  writes=("g11",))
            p.mm(ps_c1[:, :], ones_f32[:, :], g11[:, :], True, True,
                 reads=("ones", "g11"), writes=("psc1",))
            p.op("ve", lambda e: e.tensor_copy(gam_col[:, :], ps_c1[:, :]),
                 reads=("psc1",), writes=("gamc",))
            p.mm(ps_c1[:, :], ones_f32[:, :], bs11[:, :], True, True,
                 reads=("ones", "bs11", "gamc"), writes=("psc1",))
            p.op("ve", lambda e: e.tensor_copy(bvs_col[:, :], ps_c1[:, :]),
                 reads=("psc1",), writes=("bvsc",))
            p.play()
            spctx.close()

            with ExitStack() as actx:
                SA = actx.enter_context
                wv_b = SA(nc.sbuf_tensor("wv_b", [128, DC * D], BF16))
                A0 = SA(nc.sbuf_tensor("A0", [128, DC * D], F32))
                A1 = SA(nc.sbuf_tensor("A1", [128, DC * D], BF16))

                pctx = ExitStack()
                if True:
                    SP = pctx.enter_context
                    wqt_b = SP(nc.sbuf_tensor("wqt_b", [128, DC * D], BF16))
                    wkt_b = SP(nc.sbuf_tensor("wkt_b", [128, DC * D], BF16))
                    q0 = SP(nc.sbuf_tensor("q0", [128, 4 * D], BF16))
                    k0 = SP(nc.sbuf_tensor("k0", [128, 4 * D], BF16))
                    sqb = SP(nc.sbuf_tensor("sqb", [128, D], BF16))
                    wld = [SP(nc.sbuf_tensor("wld0", [128, D], F32))] * 2
                    sgn2 = SP(nc.sbuf_tensor("sgn2", [128, D], BF16))
                    sgl = [sgn, sgn2]
                    sgk_l = ["sgn", "sgnB"]
                    ps_tr = SP(nc.psum_tensor("ps_tr", [128, D], BF16))
                    ps_q = SP(nc.psum_tensor("ps_q", [128, D], F32))
                    ps_k = SP(nc.psum_tensor("ps_k", [128, D], F32))
                    ps_sm = SP(nc.psum_tensor("ps_sm", [128, 24], F32))
                    ps_a0 = [SP(nc.psum_tensor("ps_a00", [128, D], F32))]

                    for wi, (dst, src, dk) in enumerate((
                            (wqt_b, wqt_d, "wqtb"), (wkt_b, wkt_d, "wktb"),
                            (wv_b, wv_d, "wvb"))):
                        for dc in range(DC):
                            par = 0
                            wk = "wld0"
                            p.dma(wld[par][:, :],
                                  src[dc * 128:(dc + 1) * 128, :],
                                  writes=(wk,))
                            p.op("sc", lambda e, dst=dst, dc=dc, par=par:
                                 e.copy(dst[:, dc * D:(dc + 1) * D],
                                        wld[par][:, :]),
                                 reads=(wk,), writes=(dk,))
                    p.op("ve", lambda e: e.memset(A0[:, :], 0.0),
                         writes=("A0",))

                    first_small = [True]

                    def proj(ps_x, pskey, wt_b, wkey, b_row, bkey,
                             dst_bf, dkey, sq_col, qc_col, close):
                        for k in range(DC):
                            for hh in range(2):
                                p.mm(ps_x[:, hh * 512:(hh + 1) * 512],
                                     xnfm[:, k * 128:(k + 1) * 128],
                                     wt_b[:, k * D + hh * 512:
                                          k * D + (hh + 1) * 512],
                                     start=(k == 0), stop=False,
                                     reads=("xnfm", wkey), writes=(pskey,))
                        for hh in range(2):
                            p.mm(ps_x[:, hh * 512:(hh + 1) * 512],
                                 ones_row[:, :],
                                 b_row[:, hh * 512:(hh + 1) * 512],
                                 start=False, stop=True,
                                 reads=("ones", bkey), writes=(pskey,))
                        p.op("sc", lambda e, ps_x=ps_x: e.copy(
                            dst_bf, ps_x[:, :]),
                            reads=(pskey,), writes=(dkey,))
                        p.op("sc", lambda e, ps_x=ps_x: e.activation(
                            sqb[:, :], ps_x[:, :], AF.Square),
                            reads=(pskey,), writes=("sqb2",))
                        for k in range(DC):
                            st = first_small[0]
                            first_small[0] = False
                            p.mm(ps_sm[:, sq_col + k:sq_col + k + 1],
                                 sqb[:, k * 128:(k + 1) * 128],
                                 ones_col[:, :], start=st,
                                 stop=(close and qc_col is None
                                       and k == DC - 1),
                                 reads=("sqb2", "ones"), writes=("pssm",))
                        if qc_col is not None:
                            for k in range(DC):
                                p.mm(ps_sm[:, qc_col + k:qc_col + k + 1],
                                     dst_bf[:, k * 128:(k + 1) * 128],
                                     ones_col[:, :], start=False,
                                     stop=False,
                                     reads=(dkey, "ones"), writes=("pssm",))

                    def _wq_units():
                        units = []
                        jc = 0
                        for wsrc, wdst in ((wg_d, wgq_sp), (wu_d, wuq_sp)):
                            for ic in range(IC):
                                b = jc % 2
                                jc += 1
                                units.append(
                                    (wsrc[ic * 128:(ic + 1) * 128, :],
                                     b * 1024, D, D, b * 1024, b,
                                     wdst[ic * 128:(ic + 1) * 128, :],
                                     b * 1024, D))
                        for hc in range(DC):
                            wsrc = wd_d[hc * 128:(hc + 1) * 128, :]
                            wdst = wdq_sp[hc * 128:(hc + 1) * 128, :]
                            for pi, (w, c0) in enumerate(
                                    ((1024, 0), (1024, 1024), (768, 2048))):
                                units.append(
                                    (wsrc if pi == 0 else None, 0, I, w,
                                     c0, pi % 2,
                                     wdst if pi == 2 else None, 0, I))
                        units = [u if k < 62 else
                                 (u[0], u[1], u[2], u[3], u[4], 0, u[6],
                                  u[7], u[8])
                                 for k, u in enumerate(units)]
                        return units

                    _units = _wq_units()
                    _ucur = [0]
                    _upre = [0]

                    def pump_pre():
                        k = _upre[0]
                        if k >= len(_units):
                            return
                        src_, lc0, lF, w, c0, sb, dst, sc0, sF = _units[k]
                        if src_ is not None:
                            p.dma(Xb[:, lc0:lc0 + lF], src_,
                                  writes=xb_bands(lc0, lF))
                        emit_wq_pre(w, c0, sgl[sb], sgk_l[sb])
                        _upre[0] = k + 1

                    def pump_main():
                        k = _ucur[0]
                        if k >= len(_units) or k >= _upre[0]:
                            return
                        src_, lc0, lF, w, c0, sb, dst, sc0, sF = _units[k]
                        emit_wq_main(w, c0, sgl[sb], sgk_l[sb])
                        if dst is not None:
                            p.dma(dst, t2[:, sc0:sc0 + sF],
                                  reads=t_bands("t2", sc0, sF), queue="po")
                        _ucur[0] = k + 1

                    p.dma(x_t[0][:, :], xb_d[0:128, :], writes=("x0",))
                    stats_tile(0, 0, sqb[:, :], "sqb2")

                    for i in range(NCC):
                        cc = i % 4
                        par = i % 2
                        if i + 1 < NCC:
                            p.dma(x_t[(i + 1) % 2][:, :],
                                  xb_d[(i + 1) * 128:(i + 2) * 128, :],
                                  writes=(f"x{(i + 1) % 2}",))
                        apply_ln(par, i, to_f32=False)
                        for k in range(DC):
                            p.tr(ps_tr[:, k * 128:(k + 1) * 128],
                                 xn_bf[:, k * 128:(k + 1) * 128],
                                 idt[:, :],
                                 reads=("xnbf", "idt"), writes=("pstr",))
                        p.op("sc", lambda e: e.copy(xnfm[:, :],
                                                    ps_tr[:, :]),
                             reads=("pstr",), writes=("xnfm",))
                        if i >= 1:
                            pump_pre()
                        proj(ps_q, "psq", wqt_b, "wqtb", bq_row, "bqr",
                             q0[:, cc * D:(cc + 1) * D], f"q0{cc}", 0, 16,
                             close=False)
                        if i >= 1:
                            pump_main()
                            pump_pre()
                        proj(ps_k, "psk", wkt_b, "wktb", bk_row, "bkr",
                             k0[:, cc * D:(cc + 1) * D], f"k0{cc}", 8, None,
                             close=(i == NCC - 1))
                        if i >= 1:
                            pump_main()
                        if i + 1 < NCC:
                            stats_tile((i + 1) % 2, i + 1, sqb[:, :],
                                       "sqb2")
                        if cc == 3:
                            for dc in range(DC):
                                pa = 0
                                for c4 in range(4):
                                    for hh in range(2):
                                        p.mm(ps_a0[pa][:, hh * 512:
                                                       (hh + 1) * 512],
                                             k0[:, c4 * D + dc * 128:
                                                c4 * D + (dc + 1) * 128],
                                             q0[:, c4 * D + hh * 512:
                                                c4 * D + (hh + 1) * 512],
                                             start=(c4 == 0), stop=(c4 == 3),
                                             reads=(f"k0{c4}", f"q0{c4}"),
                                             writes=(f"psa{pa}",))
                                p.op("ve", lambda e, dc=dc, pa=pa:
                                     e.tensor_tensor(
                                         A0[:, dc * D:(dc + 1) * D],
                                         A0[:, dc * D:(dc + 1) * D],
                                         ps_a0[pa][:, :], ALU.add),
                                     reads=("A0", f"psa{pa}"),
                                     writes=("A0",))
                    p.op("ve", lambda e: e.tensor_copy(small[:, :],
                                                       ps_sm[:, :]),
                         reads=("pssm",), writes=("small",))
                    p.dma(cc_in_d[:, 0:DC * D], A0[:, :], reads=("A0",),
                          writes=("ccin",), queue="po")
                    p.dma(cc_in_d[:, DC * D:DC * D + 24], small[:, :],
                          reads=("small",), writes=("ccin",), queue="po")
                    p.op("po", lambda e: e.collective_compute(
                        "AllReduce", ALU.add, replica_groups=cc_groups,
                        ins=[cc_in_d.ap().opt()],
                        outs=[cc_out_d.ap().opt()]),
                        reads=("ccin",), writes=("ccout",))
                pctx.close()

                wgq = A(nc.sbuf_tensor("wgq", [128, DC * I], BF16,
                                       side="right"))
                wgqv = wgq[:, :].rearrange("p (s c) -> p s c", c=I)

                with ExitStack() as ectx:
                    SE = ectx.enter_context
                    rt = [SE(nc.sbuf_tensor(f"rt{i}", [128, D], BF16))
                          for i in (0, 1)]
                    ps_trE = SE(nc.psum_tensor("ps_trE", [128, D], BF16))

                    def reload_tile(spill, F, idx, tt, tk, dstv, tag, ps,
                                    pskey):
                        p.dma(tt[:, 0:F],
                              spill[idx * 128:(idx + 1) * 128, :],
                              writes=tk)
                        nsub = F // 128
                        for s0 in range(0, nsub, 8):
                            sn = min(8, nsub - s0)
                            for k in range(sn):
                                p.tr(ps[:, k * 128:(k + 1) * 128],
                                     tt[:, (s0 + k) * 128:(s0 + k + 1) * 128],
                                     idt[:, :],
                                     reads=tk + ("idt",), writes=(pskey,))
                            p.op("sc", lambda e, s0=s0, sn=sn, idx=idx,
                                 dstv=dstv, ps=ps: e.copy(
                                     dstv[:, s0:s0 + sn,
                                          idx * 128:(idx + 1) * 128],
                                     ps[:, 0:sn * 128].rearrange(
                                         "p (s c) -> p s c", c=128)),
                                 reads=(pskey,), writes=(tag,))

                    p.op("sy", lambda e: e.dma_start(
                        out=rt[0][:, :], in_=wgq_sp[0:128, :]),
                        reads=("_dmA",),
                        writes=("rt0", "wqtb", "wktb", "_dmA"),
                        inc=16, sem="sy")
                    nsub0 = D // 128
                    for k in range(nsub0):
                        p.tr(ps_trE[:, k * 128:(k + 1) * 128],
                             rt[0][:, k * 128:(k + 1) * 128], idt[:, :],
                             reads=("rt0", "idt"), writes=("pstrE",))
                    p.op("sc", lambda e: e.copy(
                        wgqv[:, 0:nsub0, 0:128],
                        ps_trE[:, 0:nsub0 * 128].rearrange(
                            "p (s c) -> p s c", c=128)),
                        reads=("pstrE",), writes=("wgq",))
                    for icx in range(1, IC):
                        reload_tile(wgq_sp, D, icx, rt[icx % 2],
                                    (f"rt{icx % 2}",), wgqv, "wgq",
                                    ps_trE, "pstrE")
                    while _ucur[0] < len(_units):
                        pump_pre()
                        pump_main()
                    p.dma(A0[:, :], cc_out_d[:, 0:DC * D],
                          reads=("ccout",), writes=("A0",), queue="po")
                    p.dma(small[:, :], cc_out_d[:, DC * D:DC * D + 24],
                          reads=("ccout",), writes=("small",), queue="po")
                    p.play()

                with ExitStack() as bctx:
                    SB = bctx.enter_context
                    nk_c = SB(nc.sbuf_tensor("nk_c", [128, 8], F32))
                    w1_c = SB(nc.sbuf_tensor("w1_c", [128, 8], F32))
                    bt_c = SB(nc.sbuf_tensor("bt_c", [128, 8], F32))
                    tl_c = SB(nc.sbuf_tensor("tl_c", [128, 8], F32))
                    sig_c = SB(nc.sbuf_tensor("sig_c", [128, 8], F32))
                    isg_c = SB(nc.sbuf_tensor("isg_c", [128, 8], F32))
                    dtmp = SB(nc.sbuf_tensor("dtmp", [128, 128], BF16))
                    colpk = SB(nc.sbuf_tensor("colpk", [128, 24], BF16))
                    rowuk = SB(nc.sbuf_tensor("rowuk", [24, 128], BF16))
                    ps_c8 = SB(nc.psum_tensor("ps_c8", [128, 8], F32))
                    ps_g1 = SB(nc.psum_tensor("ps_g1", [128, D], F32))
                    ps_t24 = SB(nc.psum_tensor("ps_t24", [24, 128], BF16))

                    p.op("sc", lambda e: e.activation(nq_c[:, :],
                                                      small[:, 0:8],
                                                      AF.Sqrt),
                         reads=("small",), writes=("nqc",))
                    p.op("ve", lambda e: e.reciprocal(inq_c[:, :],
                                                      nq_c[:, :]),
                         reads=("nqc",), writes=("inqc",))
                    p.op("sc", lambda e: e.activation(nk_c[:, :],
                                                      small[:, 8:16],
                                                      AF.Sqrt),
                         reads=("small",), writes=("nkc",))
                    p.op("ve", lambda e: e.reciprocal(ink_c[:, :],
                                                      nk_c[:, :]),
                         reads=("nkc",), writes=("inkc",))
                    for dc in range(DC):
                        p.op("ve", lambda e, dc=dc: e.tensor_scalar(
                            A1[:, dc * D:(dc + 1) * D],
                            A0[:, dc * D:(dc + 1) * D],
                            ink_c[:, dc:dc + 1], None, ALU.mult),
                            reads=("A0", "inkc"), writes=("A1",))
                    for nch in range(DC):
                        for dc in range(DC):
                            p.mm(ps_c8[:, nch:nch + 1],
                                 A1[:, dc * D + nch * 128:
                                    dc * D + (nch + 1) * 128],
                                 ones_col[:, :], start=(dc == 0),
                                 stop=(dc == 7),
                                 reads=("A1", "ones"), writes=("psc8",))
                    p.op("ve", lambda e: e.tensor_copy(w1_c[:, :],
                                                       ps_c8[:, :]),
                         reads=("psc8",), writes=("w1c",))
                    for nch in range(DC):
                        for dc in range(DC):
                            p.mm(ps_c8[:, nch:nch + 1],
                                 A1[:, dc * D + nch * 128:
                                    dc * D + (nch + 1) * 128],
                                 bv_col[:, dc:dc + 1], start=(dc == 0),
                                 stop=(dc == 7),
                                 reads=("A1", "bvcol", "w1c"),
                                 writes=("psc8",))
                    p.op("ve", lambda e: e.tensor_copy(bt_c[:, :],
                                                       ps_c8[:, :]),
                         reads=("psc8",), writes=("btc",))
                    for ec in range(DC):
                        for dc in range(DC):
                            p.mm(ps_c8[:, ec:ec + 1],
                                 wv_b[:, dc * D + ec * 128:
                                      dc * D + (ec + 1) * 128],
                                 ones_col[:, :], start=(dc == 0),
                                 stop=(dc == 7),
                                 reads=("wvb", "ones", "btc"),
                                 writes=("psc8",))
                    p.op("ve", lambda e: e.tensor_copy(wvs_c[:, :],
                                                       ps_c8[:, :]),
                         reads=("psc8",), writes=("wvsc",))
                    p.op("ve", lambda e: e.tensor_scalar(tl_c[:, :],
                                                         small[:, 16:24],
                                                         EPS, None,
                                                         ALU.mult),
                         reads=("small",), writes=("tlc",))
                    p.op("ve", lambda e: e.tensor_tensor(tl_c[:, :],
                                                         tl_c[:, :],
                                                         w1_c[:, :],
                                                         ALU.add),
                         reads=("tlc", "w1c"), writes=("tlc",))
                    p.op("ve", lambda e: e.tensor_tensor(tl_c[:, :],
                                                         tl_c[:, :],
                                                         inq_c[:, :],
                                                         ALU.mult),
                         reads=("tlc", "inqc"), writes=("tlc",))
                    p.op("ve", lambda e: e.tensor_scalar(tl_c[:, :],
                                                         tl_c[:, :],
                                                         float(D), None,
                                                         ALU.add),
                         reads=("tlc",), writes=("tlc",))
                    p.op("ve", lambda e: e.reciprocal(tl_c[:, :],
                                                      tl_c[:, :]),
                         reads=("tlc",), writes=("tlc",))
                    p.op("ve", lambda e: e.tensor_tensor(sig_c[:, :],
                                                         tl_c[:, :],
                                                         inq_c[:, :],
                                                         ALU.mult),
                         reads=("tlc", "inqc"), writes=("sigc",))
                    p.op("ve", lambda e: e.tensor_scalar(sig_c[:, :],
                                                         sig_c[:, :],
                                                         gam_col[:, 0:1],
                                                         None, ALU.mult),
                         reads=("sigc", "gamc"), writes=("sigc",))
                    p.op("ve", lambda e: e.tensor_copy(colpk[:, 0:8],
                                                       nq_c[:, :]),
                         reads=("nqc",), writes=("colpk",))
                    p.op("ve", lambda e: e.tensor_copy(colpk[:, 8:16],
                                                       bt_c[:, :]),
                         reads=("btc", "colpk"), writes=("colpk",))
                    p.op("ve", lambda e: e.tensor_copy(colpk[:, 16:24],
                                                       sig_c[:, :]),
                         reads=("sigc", "colpk"), writes=("colpk",))
                    p.tr(ps_t24[:, :], colpk[:, :], idt[:, :],
                         reads=("colpk", "idt"), writes=("pst24",))
                    p.op("ve", lambda e: e.tensor_copy(rowuk[:, :],
                                                       ps_t24[:, :]),
                         reads=("pst24",), writes=("rowuk",))
                    for dc in range(8):
                        p.dma(nq_row[0:1, dc * 128:(dc + 1) * 128],
                              rowuk[dc:dc + 1, :],
                              reads=("rowuk",), writes=("nqrow",))
                        p.dma(rs_row[0:1, dc * 128:(dc + 1) * 128],
                              rowuk[8 + dc:9 + dc, :],
                              reads=("rowuk",), writes=("rsrow",))
                        p.dma(sg_row[0:1, dc * 128:(dc + 1) * 128],
                              rowuk[16 + dc:17 + dc, :],
                              reads=("rowuk",), writes=("sgrow",))
                    for hh in range(2):
                        p.mm(ps_g1[:, hh * 512:(hh + 1) * 512],
                             ones_row[:, :],
                             sg_row[:, hh * 512:(hh + 1) * 512],
                             start=True, stop=True,
                             reads=("ones", "sgrow"), writes=("psg1",))
                    p.op("ve", lambda e: e.tensor_copy(sig_bc[:, :],
                                                       ps_g1[:, :]),
                         reads=("psg1",), writes=("sigbc",))
                    for ec in range(DC):
                        for dc in range(DC):
                            for hh in range(2):
                                p.mm(ps_g1[:, hh * 512:(hh + 1) * 512],
                                     wv_b[:, dc * D + ec * 128:
                                          dc * D + (ec + 1) * 128],
                                     A1[:, dc * D + hh * 512:
                                        dc * D + (hh + 1) * 512],
                                     start=(dc == 0), stop=(dc == 7),
                                     reads=("wvb", "A1", "sigbc"),
                                     writes=("psg1",))
                        p.op("ve", lambda e, ec=ec: e.tensor_copy(
                            g1t[:, ec * D:(ec + 1) * D], ps_g1[:, :]),
                            reads=("psg1",), writes=("g1t",))
                    p.op("ve", lambda e: e.reciprocal(isg_c[:, :],
                                                      sig_c[:, :]),
                         reads=("sigc",), writes=("isgc",))
                    for ec in range(DC):
                        p.op("ve", lambda e, ec=ec: e.tensor_scalar(
                            dtmp[:, :], idt[:, :], isg_c[:, ec:ec + 1],
                            None, ALU.mult),
                            reads=("idt", "isgc"), writes=("dtmp",))
                        p.op("ve", lambda e, ec=ec: e.tensor_tensor(
                            g1t[:, ec * D + ec * 128:
                                ec * D + (ec + 1) * 128],
                            g1t[:, ec * D + ec * 128:
                                ec * D + (ec + 1) * 128],
                            dtmp[:, :], ALU.add),
                            reads=("g1t", "dtmp"), writes=("g1t",))
                    p.play()

        with ExitStack() as cctx:
            SC = cctx.enter_context
            wuq = SC(nc.sbuf_tensor("wuq", [128, DC * I], BF16))
            wdq = SC(nc.sbuf_tensor("wdq", [128, IC * D], BF16))
            ps_trc = SC(nc.psum_tensor("ps_trc", [128, D], BF16))
            ps_g = SC(nc.psum_tensor("ps_g", [128, 512], F32))
            ps_u = SC(nc.psum_tensor("ps_u", [128, 512], F32))
            ps_dn = SC(nc.psum_tensor("ps_dn", [128, D], F32))
            ps_P = SC(nc.psum_tensor("ps_P", [128, D], F32))
            ps_vs = SC(nc.psum_tensor("ps_vs", [1, 128], F32))
            wuqv = wuq[:, :].rearrange("p (s c) -> p s c", c=I)
            wdqv = wdq[:, :].rearrange("p (s c) -> p s c", c=D)

            def reload2(spill, F, nrows, dstv, tag):
                for icx in range(nrows // 128):
                    parb = icx % 2
                    tkp = ("t1", "t2")[parb]
                    reload_tile(spill, F, icx, (t1, t2)[parb],
                                t_bands(tkp, 0, F), dstv, tag,
                                ps_trc, "pstrc")

            reload2(wuq_sp, D, I, wuqv, "wuq")
            reload2(wdq_sp, I, D, wdqv, "wdq")

            strips = [(0, 512), (512, 512), (1024, 512), (1536, 512),
                      (2048, 512), (2560, 256)]

            def emit_strip(i0, w, si):
                psu_s, psu_k = ((ps_u, "psu") if si % 2 == 0
                                else (ps_dn, "psdn"))
                for ec in range(DC):
                    p.mm(ps_g[:, 0:w], xqfm[:, ec * 128:(ec + 1) * 128],
                         wgq[:, ec * I + i0:ec * I + i0 + w],
                         start=(ec == 0), stop=(ec == 7),
                         reads=("xqfm", "wgq"), writes=("psg",))
                for ec in range(DC):
                    p.mm(psu_s[:, 0:w], xqfm[:, ec * 128:(ec + 1) * 128],
                         wuq[:, ec * I + i0:ec * I + i0 + w],
                         start=(ec == 0), stop=(ec == 7),
                         reads=("xqfm", "wuq"), writes=(psu_k,))
                bnd = xb_bands(i0, w)
                p.op("sc", lambda e, i0=i0, w=w: e.activation(
                    Xb[:, i0:i0 + w], ps_g[:, 0:w], AF.Silu),
                    reads=("psg",), writes=bnd)
                p.op("ve", lambda e, i0=i0, w=w, psu_s=psu_s:
                     e.tensor_tensor(Xb[:, i0:i0 + w], Xb[:, i0:i0 + w],
                                     psu_s[:, 0:w], ALU.mult),
                     reads=bnd + (psu_k,), writes=bnd)

            def emit_hq_tr(part):
                blocks = range(0, 11) if part == 0 else range(11, IC)
                t2k = (t_bands("t2", 0, 1408) if part == 0
                       else t_bands("t2", 1408, 1408))
                t1k = (t_bands("t1", 0, 1408) if part == 0
                       else t_bands("t1", 1408, 1408))
                blocks = list(blocks)
                for s0 in range(0, len(blocks), 8):
                    grp = blocks[s0:s0 + 8]
                    for k, blk in enumerate(grp):
                        p.tr(ps_trc[:, k * 128:(k + 1) * 128],
                             t2[:, blk * 128:(blk + 1) * 128], idt[:, :],
                             reads=t2k + ("idt",), writes=("pstrc",))
                    for k, blk in enumerate(grp):
                        p.op("sc", lambda e, k=k, blk=blk: e.copy(
                            t1[:, blk * 128:(blk + 1) * 128],
                            ps_trc[:, k * 128:(k + 1) * 128]),
                            reads=("pstrc",), writes=t1k)

            def emit_down_a(jj):
                for icc in range(DC):
                    for hh in range(2):
                        p.mm(ps_dn[:, hh * 512:(hh + 1) * 512],
                             t1[:, icc * 128:(icc + 1) * 128],
                             wdq[:, icc * D + hh * 512:
                                 icc * D + (hh + 1) * 512],
                             start=(icc == 0), stop=False,
                             reads=t_bands("t1", 0, 1024) + ("wdq",),
                             writes=("psdn",))

            def emit_down_tail(jj, parj):
                for icc in range(DC, IC):
                    tk = t_bands("t1", icc * 128, 128)
                    for hh in range(2):
                        p.mm(ps_dn[:, hh * 512:(hh + 1) * 512],
                             t1[:, icc * 128:(icc + 1) * 128],
                             wdq[:, icc * D + hh * 512:
                                 icc * D + (hh + 1) * 512],
                             start=False, stop=(icc == IC - 1),
                             reads=tk + ("wdq",), writes=("psdn",))
                p.op("sc", lambda e: e.activation(g_bf[:, :],
                                                  ps_dn[:, :], AF.Silu),
                     reads=("psdn",), writes=("gbf",))
                p.op("po", lambda e: e.tensor_tensor(Xb[:, D:2 * D],
                                                     Xb[:, 0:D],
                                                     g_bf[:, :],
                                                     ALU.mult),
                     reads=("Xb0", "gbf"), writes=("Xb1",))
                p.op("po", lambda e, parj=parj: e.tensor_tensor(
                    Xb[:, D:2 * D], Xb[:, D:2 * D], x_t[parj][:, :],
                    ALU.add),
                    reads=(f"x{parj}",), writes=("Xb1",))
                p.dma(out_d[jj * 128:(jj + 1) * 128, :], Xb[:, D:2 * D],
                      reads=("Xb1",), writes=("OUT",), queue="po")

            p.dma(x_t[0][:, :], xb_d[0:128, :], writes=("x0",))
            for j in range(NCC):
                par = j % 2
                apply_ln(par, j, to_f32=True)
                if j > 0:
                    emit_down_a(j - 1)
                emit_quant_half(xn_f, "xnf", D, 0, filler=True)
                if j > 0:
                    emit_hq_tr(1)
                    emit_down_tail(j - 1, (j - 1) % 2)
                if j + 1 < NCC:
                    p.dma(x_t[(j + 1) % 2][:, :],
                          xb_d[(j + 1) * 128:(j + 2) * 128, :],
                          writes=(f"x{(j + 1) % 2}",))
                for k in range(DC):
                    p.tr(ps_trc[:, k * 128:(k + 1) * 128],
                         xn_bf[:, k * 128:(k + 1) * 128], idt[:, :],
                         reads=("xnbf", "idt"), writes=("pstrc",))
                p.op("sc", lambda e: e.copy(xnfm[:, :], ps_trc[:, :]),
                     reads=("pstrc",), writes=("xnfm",))
                for ec in range(DC):
                    p.mm(ps_vs[:, :], wvs_c[:, ec:ec + 1],
                         xnfm[:, ec * 128:(ec + 1) * 128],
                         start=(ec == 0), stop=(ec == 7),
                         reads=("wvsc", "xnfm"), writes=("psvs",))
                p.op("ve", lambda e: e.tensor_scalar(
                    vs_row[:, :], ps_vs[:, :], bvs_col[0:1, 0:1], None,
                    ALU.add),
                    reads=("psvs", "bvsc"), writes=("vsrow",))
                for ec in range(DC):
                    for hh in range(2):
                        p.mm(ps_P[:, hh * 512:(hh + 1) * 512],
                             xnfm[:, ec * 128:(ec + 1) * 128],
                             g1t[:, ec * D + hh * 512:
                                 ec * D + (hh + 1) * 512],
                             start=(ec == 0), stop=False,
                             reads=("xnfm", "g1t"), writes=("psP",))
                for hh in range(2):
                    p.mm(ps_P[:, hh * 512:(hh + 1) * 512], vs_row[:, :],
                         nq_row[:, hh * 512:(hh + 1) * 512], start=False,
                         stop=False,
                         reads=("vsrow", "nqrow"), writes=("psP",))
                for hh in range(2):
                    p.mm(ps_P[:, hh * 512:(hh + 1) * 512], ones_row[:, :],
                         rs_row[:, hh * 512:(hh + 1) * 512], start=False,
                         stop=True,
                         reads=("ones", "rsrow"), writes=("psP",))
                for k in range(DC):
                    p.tr(ps_trc[:, k * 128:(k + 1) * 128],
                         t2[:, k * 128:(k + 1) * 128], idt[:, :],
                         reads=t_bands("t2", 0, 1024) + ("idt",),
                         writes=("pstrc",))
                p.op("sc", lambda e: e.copy(xqfm[:, :], ps_trc[:, :]),
                     reads=("pstrc",), writes=("xqfm",))
                for si, (i0, w) in enumerate(strips):
                    emit_strip(i0, w, si)
                    if si == 2:
                        emit_quant_half(Xb, xb_bands(0, 1024), 1024, 0,
                                        filler=True)
                    elif si == 4:
                        emit_quant_half(Xb, xb_bands(1024, 1024), 1024,
                                        1024, filler=True)
                emit_quant_half(Xb, xb_bands(2048, 768), 768, 2048,
                                filler=True)
                emit_hq_tr(0)
                p.op("ve", lambda e: e.tensor_tensor(Xb[:, 0:D],
                                                     sig_bc[:, :],
                                                     ps_P[:, :], ALU.mult),
                     reads=("sigbc", "psP"), writes=("Xb0",))
            emit_down_a(NCC - 1)
            emit_hq_tr(1)
            emit_down_tail(NCC - 1, (NCC - 1) % 2)
            p.play()
        p.finish()

    return nc


_NC_CACHE = {}


def kernel(**inputs):
    x = np.ascontiguousarray(np.asarray(inputs["L0_attn_x"],
                                        dtype=np.float32))
    wq = np.asarray(inputs["L0_attn_Wq"], dtype=np.float32)
    wk = np.asarray(inputs["L0_attn_Wk"], dtype=np.float32)
    wv = np.ascontiguousarray(np.asarray(inputs["L0_attn_Wv"],
                                         dtype=np.float32))
    bq = np.ascontiguousarray(np.asarray(inputs["L0_attn_bq"],
                                         dtype=np.float32))
    bk = np.ascontiguousarray(np.asarray(inputs["L0_attn_bk"],
                                         dtype=np.float32))
    bv = np.ascontiguousarray(np.asarray(inputs["L0_attn_bv"],
                                         dtype=np.float32))
    gam = np.ascontiguousarray(np.asarray(inputs["L0_attn_gamma"],
                                          dtype=np.float32))
    wg = np.ascontiguousarray(np.asarray(inputs["L0_ffn_w_gate"],
                                         dtype=np.float32))
    wu = np.ascontiguousarray(np.asarray(inputs["L0_ffn_w_up"],
                                         dtype=np.float32))
    wd = np.ascontiguousarray(np.asarray(inputs["L0_ffn_w_down"],
                                         dtype=np.float32))
    wqt = np.ascontiguousarray(wq.T)
    wkt = np.ascontiguousarray(wk.T)
    ident = np.eye(128, dtype=ml_dtypes.bfloat16)

    if "nc" not in _NC_CACHE:
        _NC_CACHE["nc"] = build()
    nc = _NC_CACHE["nc"]

    in_maps = []
    for c in range(8):
        b, h = c // 2, c % 2
        if h == 0:
            xb = np.ascontiguousarray(x[b])
        else:
            xb = np.ascontiguousarray(
                np.concatenate([x[b, TOKC:], x[b, :TOKC]], axis=0))
        in_maps.append({
            "xb": xb,
            "wqt": wqt, "wkt": wkt, "wv": wv,
            "bq": bq, "bk": bk, "bv": bv, "gam": gam,
            "wg": wg, "wu": wu, "wd": wd, "ident": ident,
        })
    res = run_bass_kernel_spmd(nc, in_maps, core_ids=list(range(8)))
    out = np.empty((4, TOKA, D), dtype=np.float32)
    for c in range(8):
        b, h = c // 2, c % 2
        out[b, h * TOKC:(h + 1) * TOKC] = res.results[c]["out"]
    return out


# version 28 (final)
# speedup vs baseline: 1.0472x; 1.0238x over previous
import numpy as np
import ml_dtypes
import concourse.bass as bass
import concourse.mybir as mybir
from concourse.bass_utils import run_bass_kernel_spmd

F32 = mybir.dt.float32
BF16 = mybir.dt.bfloat16
AF = mybir.ActivationFunctionType
ALU = mybir.AluOpType
AX = mybir.AxisListType

D = 1024
I = 2816
TOKA = 8192
TOKC = 4096
NCA = TOKA // 128
NCC = TOKC // 128
DC = 8
IC = 22
EPS = 1e-6
LNEPS = 1e-5

ENGS = ("sy", "te", "ve", "sc", "po")
SEMS = ENGS + ("pd",)


class Op:
    __slots__ = ("ek", "sem", "fn", "inc", "need_inc", "semval", "waits")

    def __init__(self, ek, sem, fn, inc):
        self.ek, self.sem, self.fn, self.inc = ek, sem, fn, inc
        self.need_inc = inc == 16
        self.semval = None
        self.waits = []


class Prog:

    def __init__(self, nc):
        self.nc = nc
        self.sems = {e: nc.ctx.enter_context(nc.semaphore(f"s_{e}"))
                     for e in SEMS}
        self.ops = []
        self.counts = {e: 0 for e in SEMS}
        self.last_write = {}
        self.readers = {}
        self.fences = {}

    def op(self, ek, fn, reads=(), writes=(), inc=1, sem=None):
        ps_reads = tuple(k for k in reads if k.startswith("ps"))
        if ps_reads:
            reads = tuple(k for k in reads if not k.startswith("ps"))
            writes = tuple(writes) + ps_reads
        o = Op(ek, sem or ek, fn, inc)
        deps = []
        for k in reads:
            w = self.last_write.get(k)
            if w is not None:
                deps.append(w)
        for k in writes:
            w = self.last_write.get(k)
            if w is not None:
                deps.append(w)
            deps.extend(self.readers.get(k, {}).values())
        for d in deps:
            if d is o:
                continue
            if d.ek == "te" and ek == "te":
                continue
            if d not in o.waits:
                o.waits.append(d)
                d.need_inc = True
        for k in reads:
            self.readers.setdefault(k, {})[ek] = o
        for k in writes:
            self.last_write[k] = o
            self.readers[k] = {}
        self.ops.append(o)
        return o

    def dma(self, out, in_, reads=(), writes=(), queue="sy"):
        if queue == "sy":
            chain, sem = "_dmA", "sy"
        else:
            chain, sem = "_dmB", "pd"
        return self.op(queue, lambda e: e.dma_start(out=out, in_=in_),
                       reads=tuple(reads) + (chain,),
                       writes=tuple(writes) + (chain,), inc=16, sem=sem)

    def mm(self, out, lhsT, rhs, start, stop, reads=(), writes=()):
        return self.op("te", lambda e: e.matmul(out, lhsT, rhs, start=start,
                                                stop=stop),
                       reads=reads, writes=writes)

    def tr(self, out, in_, ident, reads=(), writes=()):
        return self.op("te", lambda e: e.transpose(out, in_, ident),
                       reads=reads, writes=writes)

    def play(self):
        ops, self.ops = self.ops, []
        if not ops:
            return
        lastop = {}
        for o in ops:
            lastop[o.sem] = o
        for o in lastop.values():
            o.need_inc = True
        for o in ops:
            if o.need_inc:
                self.counts[o.sem] += o.inc
                o.semval = self.counts[o.sem]
        with self.nc.Block() as block:
            sections = {"sy": block.sync, "te": block.tensor,
                        "ve": block.vector, "sc": block.scalar,
                        "po": block.gpsimd}
            for ek, section in sections.items():
                mine = [o for o in ops if o.ek == ek]
                if not mine:
                    continue
                sems = self.sems

                def body(eng, mine=mine, sems=sems):
                    water = {e: 0 for e in SEMS}
                    for o in mine:
                        for d in o.waits:
                            v = d.semval
                            assert v is not None
                            if v > water[d.sem]:
                                eng.wait_ge(sems[d.sem], v)
                                water[d.sem] = v
                        ins = o.fn(eng)
                        if o.need_inc:
                            ins.then_inc(sems[o.sem], o.inc)

                section(body)
        fence = dict(lastop)
        for k in list(self.last_write):
            w = self.last_write[k]
            self.last_write[k] = fence.get(w.sem, w)
        for k in list(self.readers):
            self.readers[k] = {e: fence.get(o.sem, o)
                               for e, o in self.readers[k].items()}

    def finish(self):
        counts = dict(self.counts)
        with self.nc.Block() as block:
            def body(eng):
                for e in SEMS:
                    if counts[e]:
                        eng.wait_ge(self.sems[e], counts[e])
            block.sync(body)


def t_bands(pfx, c0, w):
    bands = []
    for b, lo, hi in ((pfx + "a", 0, 1024), (pfx + "b", 1024, 2048),
                      (pfx + "c", 2048, 2816)):
        if c0 < hi and c0 + w > lo:
            bands.append(b)
    return tuple(bands)


def xb_bands(c0, w):
    bands = []
    for b, lo, hi in (("Xb0", 0, 1024), ("Xb1", 1024, 2048),
                      ("Xb2b", 2048, 2816)):
        if c0 < hi and c0 + w > lo:
            bands.append(b)
    return tuple(bands)


def build(cc_groups=((0, 1), (2, 3), (4, 5), (6, 7))):
    cc_groups = [list(g) for g in cc_groups]
    nc = bass.Bass("TRN2", target_bir_lowering=False)

    xb_d = nc.dram_tensor("xb", [TOKA, D], F32, kind="ExternalInput")
    wqt_d = nc.dram_tensor("wqt", [D, D], F32, kind="ExternalInput")
    wkt_d = nc.dram_tensor("wkt", [D, D], F32, kind="ExternalInput")
    wv_d = nc.dram_tensor("wv", [D, D], F32, kind="ExternalInput")
    bq_d = nc.dram_tensor("bq", [D], F32, kind="ExternalInput")
    bk_d = nc.dram_tensor("bk", [D], F32, kind="ExternalInput")
    bv_d = nc.dram_tensor("bv", [D], F32, kind="ExternalInput")
    gam_d = nc.dram_tensor("gam", [1], F32, kind="ExternalInput")
    wg_d = nc.dram_tensor("wg", [I, D], F32, kind="ExternalInput")
    wu_d = nc.dram_tensor("wu", [I, D], F32, kind="ExternalInput")
    wd_d = nc.dram_tensor("wd", [D, I], F32, kind="ExternalInput")
    ident_d = nc.dram_tensor("ident", [128, 128], BF16, kind="ExternalInput")
    out_d = nc.dram_tensor("out", [TOKC, D], F32, kind="ExternalOutput")
    wgq_sp = nc.dram_tensor("wgq_sp", [I, D], BF16, kind="Internal")
    wuq_sp = nc.dram_tensor("wuq_sp", [I, D], BF16, kind="Internal")
    wdq_sp = nc.dram_tensor("wdq_sp", [D, I], BF16, kind="Internal")
    cc_in_d = nc.dram_tensor("cc_in", [128, DC * D + 24], F32,
                             kind="Internal")
    cc_out_d = nc.dram_tensor("cc_out", [128, DC * D + 24], F32,
                              kind="Internal")

    from contextlib import ExitStack
    with ExitStack() as ctx:
        A = ctx.enter_context

        x_t = [A(nc.sbuf_tensor(f"x_t{i}", [128, D], F32)) for i in (0, 1)]
        xn_f = A(nc.sbuf_tensor("xn_f", [128, D], F32))
        xn_bf = A(nc.sbuf_tensor("xn_bf", [128, D], BF16))
        xnfm = A(nc.sbuf_tensor("xnfm", [128, D], BF16))
        xqfm = A(nc.sbuf_tensor("xqfm", [128, D], BF16))
        Xb = A(nc.sbuf_tensor("Xb", [128, I], F32))
        u_b = A(nc.sbuf_tensor("u_b", [128, D], F32))
        t1 = A(nc.sbuf_tensor("t1", [128, I], BF16))
        t2 = A(nc.sbuf_tensor("t2", [128, I], BF16))
        sgn = A(nc.sbuf_tensor("sgn", [128, D], BF16))
        g_bf = A(nc.sbuf_tensor("g_bf", [128, D], BF16))
        r16 = A(nc.sbuf_tensor("r16", [128, I // 16], F32))
        scl = A(nc.sbuf_tensor("scl", [128, I // 16], F32))
        mcol = A(nc.sbuf_tensor("mcol", [128, NCC], F32))
        rcol = A(nc.sbuf_tensor("rcol", [128, NCC], F32))
        sm1 = A(nc.sbuf_tensor("sm1", [128, 1], F32))
        sm2 = A(nc.sbuf_tensor("sm2", [128, 1], F32))
        sm3 = A(nc.sbuf_tensor("sm3", [128, 1], F32))
        sm4 = A(nc.sbuf_tensor("sm4", [128, 1], F32))
        gam_col = A(nc.sbuf_tensor("gam_col", [128, 1], F32))
        bvs_col = A(nc.sbuf_tensor("bvs_col", [128, 1], F32))
        idt = A(nc.sbuf_tensor("idt", [128, 128], BF16))
        ones_col = A(nc.sbuf_tensor("ones_col", [128, 1], BF16))
        ones_row = A(nc.sbuf_tensor("ones_row", [1, 128], BF16))
        bv_col = A(nc.sbuf_tensor("bv_col", [128, 8], BF16))
        small = A(nc.sbuf_tensor("small", [128, 24], F32))
        nq_c = A(nc.sbuf_tensor("nq_c", [128, 8], F32))
        inq_c = A(nc.sbuf_tensor("inq_c", [128, 8], F32))
        ink_c = A(nc.sbuf_tensor("ink_c", [128, 8], F32))
        wvs_c = A(nc.sbuf_tensor("wvs_c", [128, 8], BF16))
        nq_row = A(nc.sbuf_tensor("nq_row", [1, D], BF16))
        rs_row = A(nc.sbuf_tensor("rs_row", [1, D], BF16))
        sg_row = A(nc.sbuf_tensor("sg_row", [1, D], BF16))
        sig_bc = A(nc.sbuf_tensor("sig_bc", [128, D], BF16))
        g1t = A(nc.sbuf_tensor("g1t", [128, DC * D], BF16))
        vs_row = A(nc.sbuf_tensor("vs_row", [1, 128], BF16))

        p = Prog(nc)

        def emit_quant_half(X, Xkey, F, c0, filler=False):
            if isinstance(Xkey, str):
                Xkey = (Xkey,)
            G = F // 16
            g0 = c0 // 16
            Xs = X[:, c0:c0 + F]
            Xg = Xs.rearrange("p (g k) -> p g k", k=16)
            t1s = t1[:, c0:c0 + F]
            t2s = t2[:, c0:c0 + F]
            t1k = t_bands("t1", c0, F)
            t2k = t_bands("t2", c0, F)
            ukeys = ("u0",) if F <= 512 else ("u0", "u1")
            e_mul = "ve"
            p.op("ve", lambda e: e.reduce_max(r16[:, g0:g0 + G], Xg,
                                              axis=AX.X,
                                              apply_absolute_value=True),
                 reads=Xkey, writes=("r16",))
            p.op("ve", lambda e: e.tensor_scalar(scl[:, g0:g0 + G],
                                                 r16[:, g0:g0 + G],
                                                 1.0 / 12.0, 5e-13,
                                                 ALU.mult, ALU.max),
                 reads=("r16",), writes=("scl",))
            p.op("ve", lambda e: e.reciprocal(r16[:, g0:g0 + G],
                                              scl[:, g0:g0 + G]),
                 reads=("scl",), writes=("r16",))
            p.op("sc", lambda e: e.activation(u_b[:, 0:F], Xs, AF.Abs,
                                              scale=0.5),
                 reads=Xkey, writes=ukeys)
            p.op("sc", lambda e: e.activation(sgn[:, 0:F], Xs, AF.Sign),
                 reads=Xkey, writes=("sgn",))
            rbc = r16[:, g0:g0 + G].unsqueeze(2).broadcast_to([128, G, 16])
            ug = u_b[:, 0:F].rearrange("p (g k) -> p g k", k=16)
            p.op(e_mul, lambda e: e.tensor_tensor(ug, ug, rbc, ALU.mult),
                 reads=ukeys + ("r16",), writes=ukeys)
            uf = u_b[:, 0:F]
            p.op(e_mul, lambda e: e.tensor_scalar(t1s, uf, 5.0, 2.0,
                                                  ALU.is_gt, ALU.mult),
                 reads=ukeys, writes=t1k)
            chain = [(3.5, t1s, t2s, t1k, t2k), (2.5, t2s, t1s, t2k, t1k),
                     (None, t1s, t2s, t1k, t2k),
                     (1.75, t2s, t1s, t2k, t1k),
                     (1.25, t1s, t2s, t1k, t2k),
                     (0.75, t2s, t1s, t2k, t1k),
                     (0.25, t1s, t2s, t1k, t2k)]
            for thr, sb, db, skk, dkk in chain:
                if thr is None:
                    p.op("ve", lambda e, sb=sb, db=db: e.tensor_scalar(
                        db, sb, 2.0, None, ALU.mult),
                        reads=skk, writes=dkk)
                else:
                    p.op("ve", lambda e, thr=thr, sb=sb, db=db:
                         e.scalar_tensor_tensor(db, uf, thr, sb, ALU.is_gt,
                                                ALU.add),
                         reads=ukeys + skk, writes=dkk)
            t2g = t2s.rearrange("p (g k) -> p g k", k=16)
            e_fin = "ve"
            p.op(e_fin, lambda e: e.tensor_tensor(t2s, t2s, sgn[:, 0:F],
                                                  ALU.mult),
                 reads=t2k + ("sgn",), writes=t2k)
            sbc = scl[:, g0:g0 + G].unsqueeze(2).broadcast_to([128, G, 16])
            p.op(e_fin, lambda e: e.tensor_tensor(t2g, t2g, sbc, ALU.mult),
                 reads=t2k + ("scl",), writes=t2k)

        def emit_wq_pre(F, c0, sg, sgk):
            G = F // 16
            g0 = c0 // 16
            Xkey = xb_bands(c0, F)
            Xs = Xb[:, c0:c0 + F]
            Xg = Xs.rearrange("p (g k) -> p g k", k=16)
            p.op("ve", lambda e: e.reduce_max(r16[:, g0:g0 + G], Xg,
                                              axis=AX.X,
                                              apply_absolute_value=True),
                 reads=Xkey, writes=("r16",))
            p.op("ve", lambda e: e.tensor_scalar(scl[:, g0:g0 + G],
                                                 r16[:, g0:g0 + G],
                                                 1.0 / 12.0, 5e-13,
                                                 ALU.mult, ALU.max),
                 reads=("r16",), writes=("scl",))
            p.op("ve", lambda e: e.reciprocal(r16[:, g0:g0 + G],
                                              scl[:, g0:g0 + G]),
                 reads=("scl",), writes=("r16",))
            p.op("sc", lambda e: e.activation(u_b[:, 0:F], Xs, AF.Abs,
                                              scale=0.5),
                 reads=Xkey, writes=("u0", "u1"))
            p.op("sc", lambda e: e.activation(sg[:, 0:F], Xs, AF.Sign),
                 reads=Xkey, writes=(sgk,))

        def emit_wq_main(F, c0, sg, sgk):
            G = F // 16
            g0 = c0 // 16
            t1s = t1[:, c0:c0 + F]
            t2s = t2[:, c0:c0 + F]
            t1k = t_bands("t1", c0, F)
            t2k = t_bands("t2", c0, F)
            rbc = r16[:, g0:g0 + G].unsqueeze(2).broadcast_to([128, G, 16])
            ug = u_b[:, 0:F].rearrange("p (g k) -> p g k", k=16)
            p.op("ve", lambda e: e.tensor_tensor(ug, ug, rbc, ALU.mult),
                 reads=("u0", "u1", "r16"), writes=("u0", "u1"))
            uf = u_b[:, 0:F]
            p.op("ve", lambda e: e.tensor_scalar(t1s, uf, 5.0, 2.0,
                                                 ALU.is_gt, ALU.mult),
                 reads=("u0", "u1"), writes=t1k)
            chain = [(3.5, t1s, t2s, t1k, t2k),
                     (2.5, t2s, t1s, t2k, t1k),
                     (None, t1s, t2s, t1k, t2k),
                     (1.75, t2s, t1s, t2k, t1k),
                     (1.25, t1s, t2s, t1k, t2k),
                     (0.75, t2s, t1s, t2k, t1k),
                     (0.25, t1s, t2s, t1k, t2k)]
            for thr, sb, db, skk, dkk in chain:
                if thr is None:
                    p.op("ve", lambda e, sb=sb, db=db: e.tensor_scalar(
                        db, sb, 2.0, None, ALU.mult),
                        reads=skk, writes=dkk)
                else:
                    p.op("ve", lambda e, thr=thr, sb=sb, db=db:
                         e.scalar_tensor_tensor(db, uf, thr, sb, ALU.is_gt,
                                                ALU.add),
                         reads=("u0", "u1") + skk, writes=dkk)
            p.op("po", lambda e: e.tensor_tensor(t2s, t2s, sg[:, 0:F],
                                                 ALU.mult),
                 reads=t2k + (sgk,), writes=t2k)
            sbc = scl[:, g0:g0 + G].unsqueeze(2).broadcast_to([128, G, 16])
            t2g = t2s.rearrange("p (g k) -> p g k", k=16)
            p.op("po", lambda e: e.tensor_tensor(t2g, t2g, sbc, ALU.mult),
                 reads=t2k + ("scl",), writes=t2k)

        def stats_tile(par, idx, sq_ps, sq_ps_key):
            xt = x_t[par]
            xk = f"x{par}"
            p.op("ve", lambda e: e.reduce_sum(sm1[:, :], xt[:, :],
                                              axis=AX.X),
                 reads=(xk,), writes=("sm1",))
            p.op("ve", lambda e, idx=idx: e.tensor_scalar(
                mcol[:, idx:idx + 1], sm1[:, :], 1.0 / D, None, ALU.mult),
                reads=("sm1",), writes=("mcol",))
            p.op("ve", lambda e, idx=idx: e.tensor_scalar(
                sm2[:, :], mcol[:, idx:idx + 1], -1.0, None, ALU.mult),
                reads=("mcol",), writes=("sm2",))
            p.op("sc", lambda e: e.activation(sq_ps, xt[:, :], AF.Square,
                                              bias=sm2[:, 0:1], scale=1.0,
                                              accum_out=sm3[:, :]),
                 reads=(xk, "sm2"), writes=(sq_ps_key, "sm3"))
            p.op("ve", lambda e: e.tensor_scalar(sm3[:, :], sm3[:, :],
                                                 1.0 / D, LNEPS, ALU.mult,
                                                 ALU.add),
                 reads=("sm3",), writes=("sm3",))
            p.op("sc", lambda e: e.activation(sm4[:, :], sm3[:, :],
                                              AF.Sqrt),
                 reads=("sm3",), writes=("sm4",))
            p.op("ve", lambda e, idx=idx: e.reciprocal(
                rcol[:, idx:idx + 1], sm4[:, :]),
                reads=("sm4",), writes=("rcol",))

        def apply_ln(par, idx, to_f32):
            xt = x_t[par]
            xk = f"x{par}"
            if to_f32:
                p.op("ve", lambda e, idx=idx: e.tensor_scalar(
                    xn_f[:, :], xt[:, :], mcol[:, idx:idx + 1],
                    rcol[:, idx:idx + 1], ALU.subtract, ALU.mult),
                    reads=(xk, "mcol", "rcol"), writes=("xnf",))
                p.op("sc", lambda e: e.copy(xn_bf[:, :], xn_f[:, :]),
                     reads=("xnf",), writes=("xnbf",))
            else:
                p.op("ve", lambda e, idx=idx: e.tensor_scalar(
                    xn_bf[:, :], xt[:, :], mcol[:, idx:idx + 1],
                    rcol[:, idx:idx + 1], ALU.subtract, ALU.mult),
                    reads=(xk, "mcol", "rcol"), writes=("xnbf",))

        p.dma(idt[:, :], ident_d[:, :], writes=("idt",))
        p.op("ve", lambda e: e.memset(ones_col[:, :], 1.0),
             writes=("ones",))
        p.op("ve", lambda e: e.memset(ones_row[:, :], 1.0),
             writes=("ones",))

        with ExitStack() as sctx:
            S = sctx.enter_context
            rowf = S(nc.sbuf_tensor("rowf", [1, D], F32))
            ones_f32 = S(nc.sbuf_tensor("ones_f32", [1, 128], F32))
            bv8 = S(nc.sbuf_tensor("bv8", [8, 128], F32))
            bv8b = S(nc.sbuf_tensor("bv8b", [8, 128], BF16))
            g11 = S(nc.sbuf_tensor("g11", [1, 1], F32))
            bs11 = S(nc.sbuf_tensor("bs11", [1, 1], F32))
            bq_row = S(nc.sbuf_tensor("bq_row", [1, D], BF16))
            bk_row = S(nc.sbuf_tensor("bk_row", [1, D], BF16))
            spctx = S(ExitStack())
            ps_t8 = spctx.enter_context(nc.psum_tensor("ps_t8", [128, 8],
                                                       BF16))
            ps_c1 = spctx.enter_context(nc.psum_tensor("ps_c1", [128, 1],
                                                       F32))

            p.op("ve", lambda e: e.memset(ones_f32[:, :], 1.0),
                 writes=("ones",))
            p.dma(rowf[:, :], bq_d.ap().rearrange("(a n) -> a n", a=1),
                  writes=("rowf",))
            p.op("ve", lambda e: e.tensor_copy(bq_row[:, :], rowf[:, :]),
                 reads=("rowf",), writes=("bqr",))
            p.dma(rowf[:, :], bk_d.ap().rearrange("(a n) -> a n", a=1),
                  reads=("bqr",), writes=("rowf",))
            p.op("ve", lambda e: e.tensor_copy(bk_row[:, :], rowf[:, :]),
                 reads=("rowf",), writes=("bkr",))
            p.dma(rowf[:, :], bv_d.ap().rearrange("(a n) -> a n", a=1),
                  reads=("bkr",), writes=("rowf",))
            p.op("ve", lambda e: e.reduce_sum(bs11[:, :], rowf[:, :],
                                              axis=AX.X),
                 reads=("rowf",), writes=("bs11",))
            p.dma(bv8[:, :], bv_d.ap().rearrange("(a n) -> a n", a=8),
                  writes=("bv8",))
            p.op("ve", lambda e: e.tensor_copy(bv8b[:, :], bv8[:, :]),
                 reads=("bv8",), writes=("bv8b",))
            p.tr(ps_t8[:, :], bv8b[:, :], idt[0:8, 0:8],
                 reads=("bv8b", "idt"), writes=("pst8",))
            p.op("ve", lambda e: e.tensor_copy(bv_col[:, :], ps_t8[:, :]),
                 reads=("pst8",), writes=("bvcol",))
            p.dma(g11[:, :], gam_d.ap().rearrange("(a n) -> a n", a=1),
                  writes=("g11",))
            p.mm(ps_c1[:, :], ones_f32[:, :], g11[:, :], True, True,
                 reads=("ones", "g11"), writes=("psc1",))
            p.op("ve", lambda e: e.tensor_copy(gam_col[:, :], ps_c1[:, :]),
                 reads=("psc1",), writes=("gamc",))
            p.mm(ps_c1[:, :], ones_f32[:, :], bs11[:, :], True, True,
                 reads=("ones", "bs11", "gamc"), writes=("psc1",))
            p.op("ve", lambda e: e.tensor_copy(bvs_col[:, :], ps_c1[:, :]),
                 reads=("psc1",), writes=("bvsc",))
            p.play()
            spctx.close()

            with ExitStack() as actx:
                SA = actx.enter_context
                wv_b = SA(nc.sbuf_tensor("wv_b", [128, DC * D], BF16))
                A0 = SA(nc.sbuf_tensor("A0", [128, DC * D], F32))
                A1 = SA(nc.sbuf_tensor("A1", [128, DC * D], BF16))

                pctx = ExitStack()
                if True:
                    SP = pctx.enter_context
                    wqt_b = SP(nc.sbuf_tensor("wqt_b", [128, DC * D], BF16))
                    wkt_b = SP(nc.sbuf_tensor("wkt_b", [128, DC * D], BF16))
                    q0 = SP(nc.sbuf_tensor("q0", [128, 4 * D], BF16))
                    k0 = SP(nc.sbuf_tensor("k0", [128, 4 * D], BF16))
                    sqb = SP(nc.sbuf_tensor("sqb", [128, D], BF16))
                    wld = [SP(nc.sbuf_tensor("wld0", [128, D], F32))] * 2
                    sgn2 = SP(nc.sbuf_tensor("sgn2", [128, D], BF16))
                    sgl = [sgn, sgn2]
                    sgk_l = ["sgn", "sgnB"]
                    ps_tr = SP(nc.psum_tensor("ps_tr", [128, D], BF16))
                    ps_q = SP(nc.psum_tensor("ps_q", [128, D], F32))
                    ps_k = SP(nc.psum_tensor("ps_k", [128, D], F32))
                    ps_sm = SP(nc.psum_tensor("ps_sm", [128, 24], F32))
                    ps_a0 = [SP(nc.psum_tensor("ps_a00", [128, D], F32))]

                    for wi, (dst, src, dk) in enumerate((
                            (wqt_b, wqt_d, "wqtb"), (wkt_b, wkt_d, "wktb"),
                            (wv_b, wv_d, "wvb"))):
                        for dc in range(DC):
                            p.dma(dst[:, dc * D:(dc + 1) * D],
                                  src[dc * 128:(dc + 1) * 128, :],
                                  writes=(dk,), queue="po")
                    p.op("ve", lambda e: e.memset(A0[:, :], 0.0),
                         writes=("A0",))

                    first_small = [True]

                    def proj(ps_x, pskey, wt_b, wkey, b_row, bkey,
                             dst_bf, dkey, sq_col, qc_col, close):
                        for k in range(DC):
                            for hh in range(2):
                                p.mm(ps_x[:, hh * 512:(hh + 1) * 512],
                                     xnfm[:, k * 128:(k + 1) * 128],
                                     wt_b[:, k * D + hh * 512:
                                          k * D + (hh + 1) * 512],
                                     start=(k == 0), stop=False,
                                     reads=("xnfm", wkey), writes=(pskey,))
                        for hh in range(2):
                            p.mm(ps_x[:, hh * 512:(hh + 1) * 512],
                                 ones_row[:, :],
                                 b_row[:, hh * 512:(hh + 1) * 512],
                                 start=False, stop=True,
                                 reads=("ones", bkey), writes=(pskey,))
                        p.op("sc", lambda e, ps_x=ps_x: e.copy(
                            dst_bf, ps_x[:, :]),
                            reads=(pskey,), writes=(dkey,))
                        p.op("sc", lambda e, ps_x=ps_x: e.activation(
                            sqb[:, :], ps_x[:, :], AF.Square),
                            reads=(pskey,), writes=("sqb2",))
                        for k in range(DC):
                            st = first_small[0]
                            first_small[0] = False
                            p.mm(ps_sm[:, sq_col + k:sq_col + k + 1],
                                 sqb[:, k * 128:(k + 1) * 128],
                                 ones_col[:, :], start=st,
                                 stop=(close and qc_col is None
                                       and k == DC - 1),
                                 reads=("sqb2", "ones"), writes=("pssm",))
                        if qc_col is not None:
                            for k in range(DC):
                                p.mm(ps_sm[:, qc_col + k:qc_col + k + 1],
                                     dst_bf[:, k * 128:(k + 1) * 128],
                                     ones_col[:, :], start=False,
                                     stop=False,
                                     reads=(dkey, "ones"), writes=("pssm",))

                    def _wq_units():
                        units = []
                        jc = 0
                        for wsrc, wdst in ((wg_d, wgq_sp), (wu_d, wuq_sp)):
                            for ic in range(IC):
                                b = jc % 2
                                jc += 1
                                units.append(
                                    (wsrc[ic * 128:(ic + 1) * 128, :],
                                     b * 1024, D, D, b * 1024, b,
                                     wdst[ic * 128:(ic + 1) * 128, :],
                                     b * 1024, D))
                        for hc in range(DC):
                            wsrc = wd_d[hc * 128:(hc + 1) * 128, :]
                            wdst = wdq_sp[hc * 128:(hc + 1) * 128, :]
                            for pi, (w, c0) in enumerate(
                                    ((1024, 0), (1024, 1024), (768, 2048))):
                                units.append(
                                    (wsrc if pi == 0 else None, 0, I, w,
                                     c0, pi % 2,
                                     wdst if pi == 2 else None, 0, I))
                        units = [u if k < 62 else
                                 (u[0], u[1], u[2], u[3], u[4], 0, u[6],
                                  u[7], u[8])
                                 for k, u in enumerate(units)]
                        return units

                    _units = _wq_units()
                    _ucur = [0]
                    _upre = [0]

                    def pump_pre():
                        k = _upre[0]
                        if k >= len(_units):
                            return
                        src_, lc0, lF, w, c0, sb, dst, sc0, sF = _units[k]
                        if src_ is not None:
                            p.dma(Xb[:, lc0:lc0 + lF], src_,
                                  writes=xb_bands(lc0, lF))
                        emit_wq_pre(w, c0, sgl[sb], sgk_l[sb])
                        _upre[0] = k + 1

                    def pump_main():
                        k = _ucur[0]
                        if k >= len(_units) or k >= _upre[0]:
                            return
                        src_, lc0, lF, w, c0, sb, dst, sc0, sF = _units[k]
                        emit_wq_main(w, c0, sgl[sb], sgk_l[sb])
                        if dst is not None:
                            p.dma(dst, t2[:, sc0:sc0 + sF],
                                  reads=t_bands("t2", sc0, sF), queue="po")
                        _ucur[0] = k + 1

                    p.dma(x_t[0][:, :], xb_d[0:128, :], writes=("x0",))
                    stats_tile(0, 0, sqb[:, :], "sqb2")

                    for i in range(NCC):
                        cc = i % 4
                        par = i % 2
                        if i + 1 < NCC:
                            p.dma(x_t[(i + 1) % 2][:, :],
                                  xb_d[(i + 1) * 128:(i + 2) * 128, :],
                                  writes=(f"x{(i + 1) % 2}",))
                        apply_ln(par, i, to_f32=False)
                        for k in range(DC):
                            p.tr(ps_tr[:, k * 128:(k + 1) * 128],
                                 xn_bf[:, k * 128:(k + 1) * 128],
                                 idt[:, :],
                                 reads=("xnbf", "idt"), writes=("pstr",))
                        p.op("sc", lambda e: e.copy(xnfm[:, :],
                                                    ps_tr[:, :]),
                             reads=("pstr",), writes=("xnfm",))
                        if i >= 1:
                            pump_pre()
                        proj(ps_q, "psq", wqt_b, "wqtb", bq_row, "bqr",
                             q0[:, cc * D:(cc + 1) * D], f"q0{cc}", 0, 16,
                             close=False)
                        if i >= 1:
                            pump_main()
                            pump_pre()
                        proj(ps_k, "psk", wkt_b, "wktb", bk_row, "bkr",
                             k0[:, cc * D:(cc + 1) * D], f"k0{cc}", 8, None,
                             close=(i == NCC - 1))
                        if i >= 1:
                            pump_main()
                        if i + 1 < NCC:
                            stats_tile((i + 1) % 2, i + 1, sqb[:, :],
                                       "sqb2")
                        if cc == 3:
                            for dc in range(DC):
                                pa = 0
                                for c4 in range(4):
                                    for hh in range(2):
                                        p.mm(ps_a0[pa][:, hh * 512:
                                                       (hh + 1) * 512],
                                             k0[:, c4 * D + dc * 128:
                                                c4 * D + (dc + 1) * 128],
                                             q0[:, c4 * D + hh * 512:
                                                c4 * D + (hh + 1) * 512],
                                             start=(c4 == 0), stop=(c4 == 3),
                                             reads=(f"k0{c4}", f"q0{c4}"),
                                             writes=(f"psa{pa}",))
                                p.op("ve", lambda e, dc=dc, pa=pa:
                                     e.tensor_tensor(
                                         A0[:, dc * D:(dc + 1) * D],
                                         A0[:, dc * D:(dc + 1) * D],
                                         ps_a0[pa][:, :], ALU.add),
                                     reads=("A0", f"psa{pa}"),
                                     writes=("A0",))
                    p.op("ve", lambda e: e.tensor_copy(small[:, :],
                                                       ps_sm[:, :]),
                         reads=("pssm",), writes=("small",))
                    p.dma(cc_in_d[:, 0:DC * D], A0[:, :], reads=("A0",),
                          writes=("ccin",), queue="po")
                    p.dma(cc_in_d[:, DC * D:DC * D + 24], small[:, :],
                          reads=("small",), writes=("ccin",), queue="po")
                    p.op("po", lambda e: e.collective_compute(
                        "AllReduce", ALU.add, replica_groups=cc_groups,
                        ins=[cc_in_d.ap().opt()],
                        outs=[cc_out_d.ap().opt()]),
                        reads=("ccin",), writes=("ccout",))
                pctx.close()

                wgq = A(nc.sbuf_tensor("wgq", [128, DC * I], BF16,
                                       side="right"))
                wgqv = wgq[:, :].rearrange("p (s c) -> p s c", c=I)

                with ExitStack() as ectx:
                    SE = ectx.enter_context
                    rt = [SE(nc.sbuf_tensor(f"rt{i}", [128, D], BF16))
                          for i in (0, 1)]
                    ps_trE = SE(nc.psum_tensor("ps_trE", [128, D], BF16))

                    def reload_tile(spill, F, idx, tt, tk, dstv, tag, ps,
                                    pskey):
                        p.dma(tt[:, 0:F],
                              spill[idx * 128:(idx + 1) * 128, :],
                              writes=tk)
                        nsub = F // 128
                        for s0 in range(0, nsub, 8):
                            sn = min(8, nsub - s0)
                            for k in range(sn):
                                p.tr(ps[:, k * 128:(k + 1) * 128],
                                     tt[:, (s0 + k) * 128:(s0 + k + 1) * 128],
                                     idt[:, :],
                                     reads=tk + ("idt",), writes=(pskey,))
                            p.op("sc", lambda e, s0=s0, sn=sn, idx=idx,
                                 dstv=dstv, ps=ps: e.copy(
                                     dstv[:, s0:s0 + sn,
                                          idx * 128:(idx + 1) * 128],
                                     ps[:, 0:sn * 128].rearrange(
                                         "p (s c) -> p s c", c=128)),
                                 reads=(pskey,), writes=(tag,))

                    p.op("sy", lambda e: e.dma_start(
                        out=rt[0][:, :], in_=wgq_sp[0:128, :]),
                        reads=("_dmA",),
                        writes=("rt0", "wqtb", "wktb", "_dmA"),
                        inc=16, sem="sy")
                    nsub0 = D // 128
                    for k in range(nsub0):
                        p.tr(ps_trE[:, k * 128:(k + 1) * 128],
                             rt[0][:, k * 128:(k + 1) * 128], idt[:, :],
                             reads=("rt0", "idt"), writes=("pstrE",))
                    p.op("sc", lambda e: e.copy(
                        wgqv[:, 0:nsub0, 0:128],
                        ps_trE[:, 0:nsub0 * 128].rearrange(
                            "p (s c) -> p s c", c=128)),
                        reads=("pstrE",), writes=("wgq",))
                    for icx in range(1, IC):
                        reload_tile(wgq_sp, D, icx, rt[icx % 2],
                                    (f"rt{icx % 2}",), wgqv, "wgq",
                                    ps_trE, "pstrE")
                    while _ucur[0] < len(_units):
                        pump_pre()
                        pump_main()
                    p.dma(A0[:, :], cc_out_d[:, 0:DC * D],
                          reads=("ccout",), writes=("A0",), queue="po")
                    p.dma(small[:, :], cc_out_d[:, DC * D:DC * D + 24],
                          reads=("ccout",), writes=("small",), queue="po")
                    p.play()

                with ExitStack() as bctx:
                    SB = bctx.enter_context
                    nk_c = SB(nc.sbuf_tensor("nk_c", [128, 8], F32))
                    w1_c = SB(nc.sbuf_tensor("w1_c", [128, 8], F32))
                    bt_c = SB(nc.sbuf_tensor("bt_c", [128, 8], F32))
                    tl_c = SB(nc.sbuf_tensor("tl_c", [128, 8], F32))
                    sig_c = SB(nc.sbuf_tensor("sig_c", [128, 8], F32))
                    isg_c = SB(nc.sbuf_tensor("isg_c", [128, 8], F32))
                    dtmp = SB(nc.sbuf_tensor("dtmp", [128, 128], BF16))
                    colpk = SB(nc.sbuf_tensor("colpk", [128, 24], BF16))
                    rowuk = SB(nc.sbuf_tensor("rowuk", [24, 128], BF16))
                    ps_c8 = SB(nc.psum_tensor("ps_c8", [128, 8], F32))
                    ps_g1 = SB(nc.psum_tensor("ps_g1", [128, D], F32))
                    ps_t24 = SB(nc.psum_tensor("ps_t24", [24, 128], BF16))

                    p.op("sc", lambda e: e.activation(nq_c[:, :],
                                                      small[:, 0:8],
                                                      AF.Sqrt),
                         reads=("small",), writes=("nqc",))
                    p.op("ve", lambda e: e.reciprocal(inq_c[:, :],
                                                      nq_c[:, :]),
                         reads=("nqc",), writes=("inqc",))
                    p.op("sc", lambda e: e.activation(nk_c[:, :],
                                                      small[:, 8:16],
                                                      AF.Sqrt),
                         reads=("small",), writes=("nkc",))
                    p.op("ve", lambda e: e.reciprocal(ink_c[:, :],
                                                      nk_c[:, :]),
                         reads=("nkc",), writes=("inkc",))
                    for dc in range(DC):
                        p.op("ve", lambda e, dc=dc: e.tensor_scalar(
                            A1[:, dc * D:(dc + 1) * D],
                            A0[:, dc * D:(dc + 1) * D],
                            ink_c[:, dc:dc + 1], None, ALU.mult),
                            reads=("A0", "inkc"), writes=("A1",))
                    for nch in range(DC):
                        for dc in range(DC):
                            p.mm(ps_c8[:, nch:nch + 1],
                                 A1[:, dc * D + nch * 128:
                                    dc * D + (nch + 1) * 128],
                                 ones_col[:, :], start=(dc == 0),
                                 stop=(dc == 7),
                                 reads=("A1", "ones"), writes=("psc8",))
                    p.op("ve", lambda e: e.tensor_copy(w1_c[:, :],
                                                       ps_c8[:, :]),
                         reads=("psc8",), writes=("w1c",))
                    for nch in range(DC):
                        for dc in range(DC):
                            p.mm(ps_c8[:, nch:nch + 1],
                                 A1[:, dc * D + nch * 128:
                                    dc * D + (nch + 1) * 128],
                                 bv_col[:, dc:dc + 1], start=(dc == 0),
                                 stop=(dc == 7),
                                 reads=("A1", "bvcol", "w1c"),
                                 writes=("psc8",))
                    p.op("ve", lambda e: e.tensor_copy(bt_c[:, :],
                                                       ps_c8[:, :]),
                         reads=("psc8",), writes=("btc",))
                    for ec in range(DC):
                        for dc in range(DC):
                            p.mm(ps_c8[:, ec:ec + 1],
                                 wv_b[:, dc * D + ec * 128:
                                      dc * D + (ec + 1) * 128],
                                 ones_col[:, :], start=(dc == 0),
                                 stop=(dc == 7),
                                 reads=("wvb", "ones", "btc"),
                                 writes=("psc8",))
                    p.op("ve", lambda e: e.tensor_copy(wvs_c[:, :],
                                                       ps_c8[:, :]),
                         reads=("psc8",), writes=("wvsc",))
                    p.op("ve", lambda e: e.tensor_scalar(tl_c[:, :],
                                                         small[:, 16:24],
                                                         EPS, None,
                                                         ALU.mult),
                         reads=("small",), writes=("tlc",))
                    p.op("ve", lambda e: e.tensor_tensor(tl_c[:, :],
                                                         tl_c[:, :],
                                                         w1_c[:, :],
                                                         ALU.add),
                         reads=("tlc", "w1c"), writes=("tlc",))
                    p.op("ve", lambda e: e.tensor_tensor(tl_c[:, :],
                                                         tl_c[:, :],
                                                         inq_c[:, :],
                                                         ALU.mult),
                         reads=("tlc", "inqc"), writes=("tlc",))
                    p.op("ve", lambda e: e.tensor_scalar(tl_c[:, :],
                                                         tl_c[:, :],
                                                         float(D), None,
                                                         ALU.add),
                         reads=("tlc",), writes=("tlc",))
                    p.op("ve", lambda e: e.reciprocal(tl_c[:, :],
                                                      tl_c[:, :]),
                         reads=("tlc",), writes=("tlc",))
                    p.op("ve", lambda e: e.tensor_tensor(sig_c[:, :],
                                                         tl_c[:, :],
                                                         inq_c[:, :],
                                                         ALU.mult),
                         reads=("tlc", "inqc"), writes=("sigc",))
                    p.op("ve", lambda e: e.tensor_scalar(sig_c[:, :],
                                                         sig_c[:, :],
                                                         gam_col[:, 0:1],
                                                         None, ALU.mult),
                         reads=("sigc", "gamc"), writes=("sigc",))
                    p.op("ve", lambda e: e.tensor_copy(colpk[:, 0:8],
                                                       nq_c[:, :]),
                         reads=("nqc",), writes=("colpk",))
                    p.op("ve", lambda e: e.tensor_copy(colpk[:, 8:16],
                                                       bt_c[:, :]),
                         reads=("btc", "colpk"), writes=("colpk",))
                    p.op("ve", lambda e: e.tensor_copy(colpk[:, 16:24],
                                                       sig_c[:, :]),
                         reads=("sigc", "colpk"), writes=("colpk",))
                    p.tr(ps_t24[:, :], colpk[:, :], idt[:, :],
                         reads=("colpk", "idt"), writes=("pst24",))
                    p.op("ve", lambda e: e.tensor_copy(rowuk[:, :],
                                                       ps_t24[:, :]),
                         reads=("pst24",), writes=("rowuk",))
                    for dc in range(8):
                        p.dma(nq_row[0:1, dc * 128:(dc + 1) * 128],
                              rowuk[dc:dc + 1, :],
                              reads=("rowuk",), writes=("nqrow",))
                        p.dma(rs_row[0:1, dc * 128:(dc + 1) * 128],
                              rowuk[8 + dc:9 + dc, :],
                              reads=("rowuk",), writes=("rsrow",))
                        p.dma(sg_row[0:1, dc * 128:(dc + 1) * 128],
                              rowuk[16 + dc:17 + dc, :],
                              reads=("rowuk",), writes=("sgrow",))
                    for hh in range(2):
                        p.mm(ps_g1[:, hh * 512:(hh + 1) * 512],
                             ones_row[:, :],
                             sg_row[:, hh * 512:(hh + 1) * 512],
                             start=True, stop=True,
                             reads=("ones", "sgrow"), writes=("psg1",))
                    p.op("ve", lambda e: e.tensor_copy(sig_bc[:, :],
                                                       ps_g1[:, :]),
                         reads=("psg1",), writes=("sigbc",))
                    for ec in range(DC):
                        for dc in range(DC):
                            for hh in range(2):
                                p.mm(ps_g1[:, hh * 512:(hh + 1) * 512],
                                     wv_b[:, dc * D + ec * 128:
                                          dc * D + (ec + 1) * 128],
                                     A1[:, dc * D + hh * 512:
                                        dc * D + (hh + 1) * 512],
                                     start=(dc == 0), stop=(dc == 7),
                                     reads=("wvb", "A1", "sigbc"),
                                     writes=("psg1",))
                        p.op("ve", lambda e, ec=ec: e.tensor_copy(
                            g1t[:, ec * D:(ec + 1) * D], ps_g1[:, :]),
                            reads=("psg1",), writes=("g1t",))
                    p.op("ve", lambda e: e.reciprocal(isg_c[:, :],
                                                      sig_c[:, :]),
                         reads=("sigc",), writes=("isgc",))
                    for ec in range(DC):
                        p.op("ve", lambda e, ec=ec: e.tensor_scalar(
                            dtmp[:, :], idt[:, :], isg_c[:, ec:ec + 1],
                            None, ALU.mult),
                            reads=("idt", "isgc"), writes=("dtmp",))
                        p.op("ve", lambda e, ec=ec: e.tensor_tensor(
                            g1t[:, ec * D + ec * 128:
                                ec * D + (ec + 1) * 128],
                            g1t[:, ec * D + ec * 128:
                                ec * D + (ec + 1) * 128],
                            dtmp[:, :], ALU.add),
                            reads=("g1t", "dtmp"), writes=("g1t",))
                    p.play()

        with ExitStack() as cctx:
            SC = cctx.enter_context
            wuq = SC(nc.sbuf_tensor("wuq", [128, DC * I], BF16))
            wdq = SC(nc.sbuf_tensor("wdq", [128, IC * D], BF16))
            ps_trc = SC(nc.psum_tensor("ps_trc", [128, D], BF16))
            ps_g = SC(nc.psum_tensor("ps_g", [128, 512], F32))
            ps_u = SC(nc.psum_tensor("ps_u", [128, 512], F32))
            ps_dn = SC(nc.psum_tensor("ps_dn", [128, D], F32))
            ps_P = SC(nc.psum_tensor("ps_P", [128, D], F32))
            ps_vs = SC(nc.psum_tensor("ps_vs", [1, 128], F32))
            wuqv = wuq[:, :].rearrange("p (s c) -> p s c", c=I)
            wdqv = wdq[:, :].rearrange("p (s c) -> p s c", c=D)

            def reload2(spill, F, nrows, dstv, tag):
                for icx in range(nrows // 128):
                    parb = icx % 2
                    tkp = ("t1", "t2")[parb]
                    reload_tile(spill, F, icx, (t1, t2)[parb],
                                t_bands(tkp, 0, F), dstv, tag,
                                ps_trc, "pstrc")

            reload2(wuq_sp, D, I, wuqv, "wuq")
            reload2(wdq_sp, I, D, wdqv, "wdq")

            strips = [(0, 512), (512, 512), (1024, 512), (1536, 512),
                      (2048, 512), (2560, 256)]

            def emit_strip(i0, w, si):
                psu_s, psu_k = ((ps_u, "psu") if si % 2 == 0
                                else (ps_dn, "psdn"))
                for ec in range(DC):
                    p.mm(ps_g[:, 0:w], xqfm[:, ec * 128:(ec + 1) * 128],
                         wgq[:, ec * I + i0:ec * I + i0 + w],
                         start=(ec == 0), stop=(ec == 7),
                         reads=("xqfm", "wgq"), writes=("psg",))
                for ec in range(DC):
                    p.mm(psu_s[:, 0:w], xqfm[:, ec * 128:(ec + 1) * 128],
                         wuq[:, ec * I + i0:ec * I + i0 + w],
                         start=(ec == 0), stop=(ec == 7),
                         reads=("xqfm", "wuq"), writes=(psu_k,))
                bnd = xb_bands(i0, w)
                p.op("sc", lambda e, i0=i0, w=w: e.activation(
                    Xb[:, i0:i0 + w], ps_g[:, 0:w], AF.Silu),
                    reads=("psg",), writes=bnd)
                p.op("ve", lambda e, i0=i0, w=w, psu_s=psu_s:
                     e.tensor_tensor(Xb[:, i0:i0 + w], Xb[:, i0:i0 + w],
                                     psu_s[:, 0:w], ALU.mult),
                     reads=bnd + (psu_k,), writes=bnd)

            def emit_hq_tr(part):
                blocks = range(0, 11) if part == 0 else range(11, IC)
                t2k = (t_bands("t2", 0, 1408) if part == 0
                       else t_bands("t2", 1408, 1408))
                t1k = (t_bands("t1", 0, 1408) if part == 0
                       else t_bands("t1", 1408, 1408))
                blocks = list(blocks)
                for s0 in range(0, len(blocks), 8):
                    grp = blocks[s0:s0 + 8]
                    for k, blk in enumerate(grp):
                        p.tr(ps_trc[:, k * 128:(k + 1) * 128],
                             t2[:, blk * 128:(blk + 1) * 128], idt[:, :],
                             reads=t2k + ("idt",), writes=("pstrc",))
                    for k, blk in enumerate(grp):
                        p.op("sc", lambda e, k=k, blk=blk: e.copy(
                            t1[:, blk * 128:(blk + 1) * 128],
                            ps_trc[:, k * 128:(k + 1) * 128]),
                            reads=("pstrc",), writes=t1k)

            def emit_down_a(jj):
                for icc in range(DC):
                    for hh in range(2):
                        p.mm(ps_dn[:, hh * 512:(hh + 1) * 512],
                             t1[:, icc * 128:(icc + 1) * 128],
                             wdq[:, icc * D + hh * 512:
                                 icc * D + (hh + 1) * 512],
                             start=(icc == 0), stop=False,
                             reads=t_bands("t1", 0, 1024) + ("wdq",),
                             writes=("psdn",))

            def emit_down_tail(jj, parj):
                for icc in range(DC, IC):
                    tk = t_bands("t1", icc * 128, 128)
                    for hh in range(2):
                        p.mm(ps_dn[:, hh * 512:(hh + 1) * 512],
                             t1[:, icc * 128:(icc + 1) * 128],
                             wdq[:, icc * D + hh * 512:
                                 icc * D + (hh + 1) * 512],
                             start=False, stop=(icc == IC - 1),
                             reads=tk + ("wdq",), writes=("psdn",))
                p.op("sc", lambda e: e.activation(g_bf[:, :],
                                                  ps_dn[:, :], AF.Silu),
                     reads=("psdn",), writes=("gbf",))
                p.op("po", lambda e: e.tensor_tensor(Xb[:, D:2 * D],
                                                     Xb[:, 0:D],
                                                     g_bf[:, :],
                                                     ALU.mult),
                     reads=("Xb0", "gbf"), writes=("Xb1",))
                p.op("po", lambda e, parj=parj: e.tensor_tensor(
                    Xb[:, D:2 * D], Xb[:, D:2 * D], x_t[parj][:, :],
                    ALU.add),
                    reads=(f"x{parj}",), writes=("Xb1",))
                p.dma(out_d[jj * 128:(jj + 1) * 128, :], Xb[:, D:2 * D],
                      reads=("Xb1",), writes=("OUT",), queue="po")

            p.dma(x_t[0][:, :], xb_d[0:128, :], writes=("x0",))
            for j in range(NCC):
                par = j % 2
                apply_ln(par, j, to_f32=True)
                if j > 0:
                    emit_down_a(j - 1)
                emit_quant_half(xn_f, "xnf", D, 0, filler=True)
                if j > 0:
                    emit_hq_tr(1)
                    emit_down_tail(j - 1, (j - 1) % 2)
                if j + 1 < NCC:
                    p.dma(x_t[(j + 1) % 2][:, :],
                          xb_d[(j + 1) * 128:(j + 2) * 128, :],
                          writes=(f"x{(j + 1) % 2}",))
                for k in range(DC):
                    p.tr(ps_trc[:, k * 128:(k + 1) * 128],
                         xn_bf[:, k * 128:(k + 1) * 128], idt[:, :],
                         reads=("xnbf", "idt"), writes=("pstrc",))
                p.op("sc", lambda e: e.copy(xnfm[:, :], ps_trc[:, :]),
                     reads=("pstrc",), writes=("xnfm",))
                for ec in range(DC):
                    p.mm(ps_vs[:, :], wvs_c[:, ec:ec + 1],
                         xnfm[:, ec * 128:(ec + 1) * 128],
                         start=(ec == 0), stop=(ec == 7),
                         reads=("wvsc", "xnfm"), writes=("psvs",))
                p.op("ve", lambda e: e.tensor_scalar(
                    vs_row[:, :], ps_vs[:, :], bvs_col[0:1, 0:1], None,
                    ALU.add),
                    reads=("psvs", "bvsc"), writes=("vsrow",))
                for ec in range(DC):
                    for hh in range(2):
                        p.mm(ps_P[:, hh * 512:(hh + 1) * 512],
                             xnfm[:, ec * 128:(ec + 1) * 128],
                             g1t[:, ec * D + hh * 512:
                                 ec * D + (hh + 1) * 512],
                             start=(ec == 0), stop=False,
                             reads=("xnfm", "g1t"), writes=("psP",))
                for hh in range(2):
                    p.mm(ps_P[:, hh * 512:(hh + 1) * 512], vs_row[:, :],
                         nq_row[:, hh * 512:(hh + 1) * 512], start=False,
                         stop=False,
                         reads=("vsrow", "nqrow"), writes=("psP",))
                for hh in range(2):
                    p.mm(ps_P[:, hh * 512:(hh + 1) * 512], ones_row[:, :],
                         rs_row[:, hh * 512:(hh + 1) * 512], start=False,
                         stop=True,
                         reads=("ones", "rsrow"), writes=("psP",))
                for k in range(DC):
                    p.tr(ps_trc[:, k * 128:(k + 1) * 128],
                         t2[:, k * 128:(k + 1) * 128], idt[:, :],
                         reads=t_bands("t2", 0, 1024) + ("idt",),
                         writes=("pstrc",))
                p.op("sc", lambda e: e.copy(xqfm[:, :], ps_trc[:, :]),
                     reads=("pstrc",), writes=("xqfm",))
                for si, (i0, w) in enumerate(strips):
                    emit_strip(i0, w, si)
                    if si == 1:
                        emit_quant_half(Xb, xb_bands(0, 1024), 1024, 0,
                                        filler=True)
                    elif si == 3:
                        emit_quant_half(Xb, xb_bands(1024, 1024), 1024,
                                        1024, filler=True)
                emit_quant_half(Xb, xb_bands(2048, 768), 768, 2048,
                                filler=True)
                emit_hq_tr(0)
                p.op("ve", lambda e: e.tensor_tensor(Xb[:, 0:D],
                                                     sig_bc[:, :],
                                                     ps_P[:, :], ALU.mult),
                     reads=("sigbc", "psP"), writes=("Xb0",))
            emit_down_a(NCC - 1)
            emit_hq_tr(1)
            emit_down_tail(NCC - 1, (NCC - 1) % 2)
            p.play()
        p.finish()

    return nc


_NC_CACHE = {}


def kernel(**inputs):
    x = np.ascontiguousarray(np.asarray(inputs["L0_attn_x"],
                                        dtype=np.float32))
    wq = np.asarray(inputs["L0_attn_Wq"], dtype=np.float32)
    wk = np.asarray(inputs["L0_attn_Wk"], dtype=np.float32)
    wv = np.ascontiguousarray(np.asarray(inputs["L0_attn_Wv"],
                                         dtype=np.float32))
    bq = np.ascontiguousarray(np.asarray(inputs["L0_attn_bq"],
                                         dtype=np.float32))
    bk = np.ascontiguousarray(np.asarray(inputs["L0_attn_bk"],
                                         dtype=np.float32))
    bv = np.ascontiguousarray(np.asarray(inputs["L0_attn_bv"],
                                         dtype=np.float32))
    gam = np.ascontiguousarray(np.asarray(inputs["L0_attn_gamma"],
                                          dtype=np.float32))
    wg = np.ascontiguousarray(np.asarray(inputs["L0_ffn_w_gate"],
                                         dtype=np.float32))
    wu = np.ascontiguousarray(np.asarray(inputs["L0_ffn_w_up"],
                                         dtype=np.float32))
    wd = np.ascontiguousarray(np.asarray(inputs["L0_ffn_w_down"],
                                         dtype=np.float32))
    wqt = np.ascontiguousarray(wq.T)
    wkt = np.ascontiguousarray(wk.T)
    ident = np.eye(128, dtype=ml_dtypes.bfloat16)

    if "nc" not in _NC_CACHE:
        _NC_CACHE["nc"] = build()
    nc = _NC_CACHE["nc"]

    in_maps = []
    for c in range(8):
        b, h = c // 2, c % 2
        if h == 0:
            xb = np.ascontiguousarray(x[b])
        else:
            xb = np.ascontiguousarray(
                np.concatenate([x[b, TOKC:], x[b, :TOKC]], axis=0))
        in_maps.append({
            "xb": xb,
            "wqt": wqt, "wkt": wkt, "wv": wv,
            "bq": bq, "bk": bk, "bv": bv, "gam": gam,
            "wg": wg, "wu": wu, "wd": wd, "ident": ident,
        })
    res = run_bass_kernel_spmd(nc, in_maps, core_ids=list(range(8)))
    out = np.empty((4, TOKA, D), dtype=np.float32)
    for c in range(8):
        b, h = c // 2, c % 2
        out[b, h * TOKC:(h + 1) * TOKC] = res.results[c]["out"]
    return out
